# version 04
# baseline (speedup 1.0000x reference)
import contextlib
import os as _os
import numpy as np
import ml_dtypes

import concourse.bass as bass
import concourse.bacc as bacc
import concourse.mybir as mybir
from concourse.bass_utils import run_bass_kernel_spmd
from concourse.expressions_rust import smin, smax

F32 = mybir.dt.float32
BF16 = mybir.dt.bfloat16
AF = mybir.ActivationFunctionType
AX = mybir.AxisListType
ALU = mybir.AluOpType
ds = bass.ds

SCALE = float(1.0 / np.sqrt(96.0))
EPS = 1e-5
NEG = -1.0e30
E = 1024
KT = 8
DKV = 256
DR = 32
SHUF = [(i + 16) % 32 for i in range(32)]


class Cfg:
    def __init__(self, TK=4096, TQ=2048, H=16, DHID=4096):
        self.TK, self.TQ, self.H, self.DHID = TK, TQ, H, DHID
        self.NQT = TQ // 128
        self.NKC = TK // 512
        self.NQC = TQ // 512
        self.NTT = TK // 128
        self.KHID = DHID // 128
        self.NHH = H // 4
        self.TQH = TQ // 2
        self.NCH = self.TQH // 512
        assert TQ % 1024 == 0 and TK % 512 == 0 and H % 4 == 0
        self.SPH = sum(j // 4 + 1 for j in range(self.NQT))
        self.NB0H = sum((j // 4 + 2) // 2 for j in range(self.NQT))


def seg_before(qt):
    k = qt >> 2
    r = qt & 3
    return qt + 2 * k * k - 2 * k + r * k


def nb0_before(qt):
    m = qt >> 3
    r = qt & 7
    return qt + 4 * m * m - 4 * m + r * m


def build_nc(cfg: Cfg, phase=7):
    P6SUB = int(_os.environ.get('P6SUB', '9'))
    nc = bacc.Bacc("TRN2", target_bir_lowering=False)
    TK, TQ, H, DHID, KHID = cfg.TK, cfg.TQ, cfg.H, cfg.DHID, cfg.KHID
    NQT, NKC, NQC, NTT, NHH = cfg.NQT, cfg.NKC, cfg.NQC, cfg.NTT, cfg.NHH
    NCH, TQH = cfg.NCH, cfg.TQH
    HD = H * 64
    KQ = HD // 128
    NRT = H // 4
    SCLN = DHID // 128
    SPH, NB0H = cfg.SPH, cfg.NB0H

    din = lambda n, shp: nc.dram_tensor(n, shp, F32, kind="ExternalInput").ap()
    xfT = din("xfT", [E, TK])
    xoT = din("xoT", [E, TQ])
    wq = din("wq", [E, HD])
    wqr = din("wqr", [E, H * DR])
    wkr = din("wkr", [E, DR])
    wkvd = din("wkvd", [E, DKV])
    wku = din("wku", [DKV, HD])
    wvu = din("wvu", [DKV, HD])
    wo = din("wo", [HD, E])
    w1t = din("w1t", [E, DHID])
    w2t = din("w2t", [E, DHID])
    h1t = din("h1t", [DHID, DHID])
    h2t = din("h2t", [DHID, DHID])
    w3t = din("w3t", [DHID, E])
    s1s = din("s1s", [128, SCLN])
    s2s = din("s2s", [128, SCLN])
    h1ss = din("h1ss", [128, SCLN])
    h2ss = din("h2ss", [128, SCLN])
    s3s = din("s3s", [128, KT])
    lobo = din("lobo", [128, H])
    ident = din("ident", [128, 128])
    masktab = din("masktab", [128, 1536])
    cosq = din("cosq", [128, TQ])
    sinq = din("sinq", [128, TQ])
    cosk = din("cosk", [32, TK])
    sink = din("sink", [32, TK])
    outT = nc.dram_tensor("outT", [E, TQ], F32, kind="ExternalOutput").ap()
    x2d = nc.dram_tensor("x2d", [E, TQ], F32, kind="Internal").ap()

    ctx = contextlib.ExitStack()
    sems = {}

    def sem(name):
        if name not in sems:
            sems[name] = ctx.enter_context(nc.semaphore(name))
        return sems[name]

    def sb(n, shp, dt=BF16, st=None):
        return (st or ctx).enter_context(nc.sbuf_tensor(n, shp, dt))

    def ps(n, shp, dt=F32, st=None):
        return (st or ctx).enter_context(nc.psum_tensor(n, shp, dt))

    wld_cnt = [0]

    def wload(dst3d, src3d):
        nc.gpsimd.wait_ge(sem('wld'), wld_cnt[0])
        nc.gpsimd.dma_start(out=dst3d, in_=src3d).then_inc(sem('wld'), 16)
        wld_cnt[0] += 16
        return wld_cnt[0]

    def emit_ln_math(lnx, cc, srow, ps_stat_, ps_pq_, dve_waits, pe_waits):
        for s_, v_ in dve_waits:
            nc.vector.wait_ge(sem(s_), v_)
        nc.vector.tensor_scalar_mul(srow[0:1, 0:512], ps_stat_[0:1, 0:512], 1.0 / E)
        nc.vector.tensor_scalar_mul(srow[0:1, 512:1024], ps_stat_[0:1, 512:1024], 1.0 / E)
        nc.vector.drain()
        nc.vector.tensor_mul(srow[0:1, 1024:1536], srow[0:1, 0:512], srow[0:1, 0:512])
        nc.vector.drain()
        nc.vector.tensor_sub(srow[0:1, 512:1024], srow[0:1, 512:1024],
                             srow[0:1, 1024:1536])
        nc.vector.drain()
        nc.vector.tensor_scalar_add(srow[0:1, 512:1024], srow[0:1, 512:1024],
                                    EPS).then_inc(sem(lnx), 1)
        nc.scalar.wait_ge(sem(lnx), 4 * cc + 1)
        nc.scalar.activation(srow[0:1, 1024:1536], srow[0:1, 512:1024],
                             AF.Sqrt, bias=0.0, scale=1.0).then_inc(sem(lnx), 1)
        nc.vector.wait_ge(sem(lnx), 4 * cc + 2)
        nc.vector.reciprocal(srow[0:1, 512:1024], srow[0:1, 1024:1536])
        nc.vector.drain()
        nc.vector.tensor_mul(srow[0:1, 0:512], srow[0:1, 0:512],
                             srow[0:1, 512:1024])
        nc.vector.engine_nop().then_inc(sem(lnx), 1)
        nc.tensor.wait_ge(sem(lnx), 4 * cc + 3)
        for s_, v_ in pe_waits:
            nc.tensor.wait_ge(sem(s_), v_)
        nc.tensor.matmul(ps_pq_[:, 0:512], s_one1f[0:1, :], srow[0:1, 512:1024],
                         start=True, stop=True)
        nc.tensor.matmul(ps_pq_[:, 512:1024], s_one1f[0:1, :], srow[0:1, 0:512],
                         start=True, stop=True).then_inc(sem(lnx), 1)

    def emit_ln_chunk(cc, xst_v, xsq_v, lnout, ln_col, ln_stride,
                      ps_stat, ps_pq, s_srow, s_ones, s_one1):
        nc.scalar.wait_ge(sem('xld'), (cc + 1) * 16)
        nc.scalar.wait_ge(sem('lnapp'), cc * KT)
        nc.scalar.activation(xsq_v, xst_v, AF.Square, bias=0.0, scale=1.0) \
            .then_inc(sem('xsq'), 1)
        nc.tensor.wait_ge(sem('ones'), 1)
        nc.tensor.wait_ge(sem('xld'), (cc + 1) * 16)
        nc.tensor.wait_ge(sem('xsq'), cc + 1)
        nc.tensor.wait_ge(sem('statdn'), cc)
        for kt in range(KT):
            nc.tensor.matmul(ps_stat[0:1, 0:512], s_ones[:, 0:1],
                             xst_v[:, kt * 512:(kt + 1) * 512],
                             start=(kt == 0), stop=(kt == KT - 1))
            i2 = nc.tensor.matmul(ps_stat[0:1, 512:1024], s_ones[:, 0:1],
                                  xsq_v[:, kt * 512:(kt + 1) * 512],
                                  start=(kt == 0), stop=(kt == KT - 1))
            if kt == KT - 1:
                i2.then_inc(sem('statmm'), 1)
        nc.vector.engine_nop().then_inc(sem('statdn'), 1) if False else None
        emit_ln_math('lnxA', cc, s_srow, ps_stat, ps_pq,
                     [('statmm', cc + 1)], [('lnapp', cc * KT)])
        nc.vector.engine_nop().then_inc(sem('statdn'), 1)
        nc.vector.wait_ge(sem('lnxA'), 4 * cc + 4)
        for kt in range(KT):
            nc.vector.tensor_mul(xsq_v[:, kt * 512:(kt + 1) * 512],
                                 xst_v[:, kt * 512:(kt + 1) * 512], ps_pq[:, 0:512])
            nc.vector.drain()
            nc.vector.tensor_sub(lnout[:, ds(kt * ln_stride + ln_col, 512)],
                                 xsq_v[:, kt * 512:(kt + 1) * 512],
                                 ps_pq[:, 512:1024]).then_inc(sem('lnapp'), 1)
            nc.vector.drain()

    with ctx:
        s_id = sb("s_id", [128, 128])
        s_msk = sb("s_msk", [128, 1536])
        s_lobo = sb("s_lobo", [128, H], F32)
        s_scl = sb("s_scl", [128, 4 * SCLN + KT], F32)
        s_ones = sb("s_ones", [128, 1])
        s_one1 = sb("s_one1", [1, 128])
        s_onesf = sb("s_onesf", [128, 1], F32)
        s_one1f = sb("s_one1f", [1, 128], F32)
        s_stg = sb("s_stg", [128, 8], F32)

        nc.sync.dma_start(out=s_lobo[:], in_=lobo).then_inc(sem('cst'), 16)
        nc.sync.dma_start(out=s_scl[:, 0:SCLN], in_=s1s).then_inc(sem('cst'), 16)
        nc.sync.dma_start(out=s_scl[:, SCLN:2 * SCLN], in_=s2s).then_inc(sem('cst'), 16)
        nc.sync.dma_start(out=s_scl[:, 2 * SCLN:3 * SCLN], in_=h1ss).then_inc(sem('cst'), 16)
        nc.sync.dma_start(out=s_scl[:, 3 * SCLN:4 * SCLN], in_=h2ss).then_inc(sem('cst'), 16)
        nc.sync.dma_start(out=s_scl[:, 4 * SCLN:4 * SCLN + KT], in_=s3s).then_inc(sem('cst'), 16)
        wload(s_id[:], ident)
        wload(s_msk[:], masktab)
        nc.vector.memset(s_ones[:], 1.0)
        nc.vector.memset(s_one1[0:1, :], 1.0)
        nc.vector.memset(s_onesf[:], 1.0)
        nc.vector.memset(s_one1f[0:1, :], 1.0)
        nc.vector.engine_nop().then_inc(sem('ones'), 1)

        sact = contextlib.ExitStack()
        sact.__enter__()
        s_lat = sb("s_lat", [128, 2 * TK], BF16, sact)
        s_krf = sb("s_krf", [32, TK], BF16, sact)
        s_qc = sb("s_qc", [128, KQ * TQ], BF16, sact)
        s_qr = sb("s_qr", [128, NRT * TQ], BF16, sact)
        s_lnq = sb("s_lnq", [128, KT * TQ], BF16, sact)
        syall = contextlib.ExitStack()
        syall.__enter__()
        s_yall = syall.enter_context(
            nc.sbuf_tensor("s_yall", [128, NQT * HD], BF16, side="right"))

        def close_act():
            syall.__exit__(None, None, None)
            sact.__exit__(None, None, None)

        if True:
            p1 = contextlib.ExitStack()
            p1.__enter__()
            s_ckc = sb("s_ckc", [32, 2048], F32, p1)
            s_xst = sb("s_xst", [128, 2 * 4096], BF16, p1)
            s_xsq = sb("s_xsq", [128, 4096], BF16, p1)
            s_lnc = sb("s_lnc", [128, 2 * 4096], BF16, p1)
            s_srow = sb("s_srow", [1, 1536], F32, p1)
            s_wkvd = sb("s_wkvd", [128, KT * DKV], BF16, p1)
            s_wkr = sb("s_wkr", [128, KT * DR], BF16, p1)
            s_krt = sb("s_krt", [32, 1024], F32, p1)
            ps_stat = ps("ps_stat", [1, 1024], F32, p1)
            ps_pq = ps("ps_pq", [128, 1024], F32, p1)
            ps_a = ps("ps_a", [128, 512], F32, p1)
            ps_b = ps("ps_b", [128, 512], F32, p1)
            ps_kr = ps("ps_kr", [32, 512], F32, p1)

            wload(s_wkvd[:].rearrange("p (kt n) -> p kt n", kt=KT),
                  wkvd.rearrange("(kt p) n -> p kt n", p=128))
            wload(s_wkr[:].rearrange("p (kt n) -> p kt n", kt=KT),
                  wkr.rearrange("(kt p) n -> p kt n", p=128))
            W_KVKR = wld_cnt[0]
            nc.sync.dma_start(out=s_ckc[0:32, 0:512], in_=cosk[:, 0:512]) \
                .then_inc(sem('tbl'), 16)
            nc.sync.dma_start(out=s_ckc[0:32, 512:1024], in_=sink[:, 0:512]) \
                .then_inc(sem('tbl'), 16)

            with nc.Fori(0, NKC) as c:
                par = c & 1
                nc.gpsimd.wait_ge(sem('xld'), c * 16)
                nc.gpsimd.wait_ge(sem('lnapp'), smax(c - 1, 0) * KT)
                nc.gpsimd.dma_start(
                    out=s_xst[:, ds(par * 4096, 4096)].rearrange("p (kt n) -> p kt n", kt=KT),
                    in_=xfT[:, ds(c * 512, 512)].rearrange("(kt p) n -> p kt n", p=128),
                ).then_inc(sem('xld'), 16)
                with nc.sync.If_lt(nc.sync.to_reg(c), NKC - 1):
                    nc.sync.wait_ge(sem('krev'), c)
                    nc.sync.wait_ge(sem('tbl'), (c + 1) * 32)
                    cnx = smin(c + 1, NKC - 1)
                    nc.sync.dma_start(out=s_ckc[0:32, ds((1 - par) * 1024, 512)],
                                      in_=cosk[:, ds(cnx * 512, 512)]) \
                        .then_inc(sem('tbl'), 16)
                    nc.sync.dma_start(out=s_ckc[0:32, ds((1 - par) * 1024 + 512, 512)],
                                      in_=sink[:, ds(cnx * 512, 512)]) \
                        .then_inc(sem('tbl'), 16)
                xst_v = s_xst[:, ds(par * 4096, 4096)]
                xsq_v = s_xsq[:]
                lnc_v = s_lnc[:, ds(par * 4096, 4096)]
                emit_ln_chunk(c, xst_v, xsq_v, s_lnc, par * 4096, 512,
                              ps_stat, ps_pq, s_srow, s_ones, s_one1)
                nc.tensor.wait_ge(sem('wld'), W_KVKR)
                nc.tensor.wait_ge(sem('lnapp'), (c + 1) * KT)
                nc.tensor.wait_ge(sem('latev'), c * 2)
                for mt in range(2):
                    pst = ps_a if mt == 0 else ps_b
                    for kt in range(KT):
                        i2 = nc.tensor.matmul(
                            pst[:, 0:512],
                            s_wkvd[:, (kt * 2 + mt) * 128:(kt * 2 + mt) * 128 + 128],
                            lnc_v[:, kt * 512:(kt + 1) * 512],
                            start=(kt == 0), stop=(kt == KT - 1))
                        if kt == KT - 1:
                            i2.then_inc(sem('latmm'), 1)
                for mt in range(2):
                    pst = ps_a if mt == 0 else ps_b
                    nc.vector.wait_ge(sem('latmm'), c * 2 + mt + 1)
                    nc.vector.tensor_copy(s_lat[:, ds(mt * TK + c * 512, 512)],
                                          pst[:, 0:512]).then_inc(sem('latev'), 1)
                nc.tensor.wait_ge(sem('krev'), c)
                for kt in range(KT):
                    i2 = nc.tensor.matmul(ps_kr[0:32, 0:512],
                                          s_wkr[:, kt * DR:kt * DR + 32],
                                          lnc_v[:, kt * 512:(kt + 1) * 512],
                                          start=(kt == 0), stop=(kt == KT - 1))
                    if kt == KT - 1:
                        i2.then_inc(sem('krmm'), 1)
                nc.vector.wait_ge(sem('krmm'), c + 1)
                nc.vector.wait_ge(sem('tbl'), (c + 1) * 32)
                nc.vector.tensor_copy(s_krt[0:32, 0:512], ps_kr[0:32, 0:512])
                nc.vector.drain()
                nc.vector.stream_shuffle(s_krt[0:32, 512:1024], s_krt[0:32, 0:512], SHUF)
                nc.vector.drain()
                nc.vector.tensor_mul(s_krt[0:32, 0:512], s_krt[0:32, 0:512],
                                     s_ckc[0:32, ds(par * 1024, 512)])
                nc.vector.tensor_mul(s_krt[0:32, 512:1024], s_krt[0:32, 512:1024],
                                     s_ckc[0:32, ds(par * 1024 + 512, 512)])
                nc.vector.drain()
                nc.vector.tensor_add(s_krf[0:32, ds(c * 512, 512)],
                                     s_krt[0:32, 0:512], s_krt[0:32, 512:1024]) \
                    .then_inc(sem('krev'), 1)

            with nc.Fori(0, NQC) as c:
                cc = NKC + c
                par = cc & 1
                nc.gpsimd.wait_ge(sem('xld'), cc * 16)
                nc.gpsimd.wait_ge(sem('lnapp'), smax(cc - 1, 0) * KT)
                nc.gpsimd.dma_start(
                    out=s_xst[:, ds(par * 4096, 4096)].rearrange("p (kt n) -> p kt n", kt=KT),
                    in_=xoT[:, ds(c * 512, 512)].rearrange("(kt p) n -> p kt n", p=128),
                ).then_inc(sem('xld'), 16)
                xst_v = s_xst[:, ds(par * 4096, 4096)]
                xsq_v = s_xsq[:]
                emit_ln_chunk(cc, xst_v, xsq_v, s_lnq, c * 512, TQ,
                              ps_stat, ps_pq, s_srow, s_ones, s_one1)

            if phase < 3:
                nc.gpsimd.wait_ge(sem('lnapp'), (NKC + NQC) * KT)
                nc.gpsimd.wait_ge(sem('xld'), (NKC + NQC) * 16)
                nc.gpsimd.wait_ge(sem('latev'), NKC * 2)
                nc.gpsimd.wait_ge(sem('krev'), NKC)
                nc.gpsimd.dma_start(out=outT[:, 0:512].rearrange("(kt p) n -> p kt n", p=128),
                                    in_=s_xst[:, 0:4096].rearrange("p (kt n) -> p kt n", kt=KT)) \
                    .then_inc(sem('ostg'), 16)
                nc.gpsimd.wait_ge(sem('ostg'), 16)
                p1.__exit__(None, None, None)
                close_act()
                nc.compile()
                return nc
            p1.__exit__(None, None, None)
            p3 = contextlib.ExitStack()
            p3.__enter__()
            NLNC0 = NKC + NQC
            s_wq = sb("s_wq", [128, KT * HD], BF16, p3)
            s_wqr = sb("s_wqr", [128, KT * H * DR], BF16, p3)
            s_cqf = sb("s_cqf", [128, TQ], F32, p3)
            s_sqf = sb("s_sqf", [128, TQ], F32, p3)
            s_qrt = sb("s_qrt", [128, 1024], F32, p3)
            ps_a = ps("ps_a3", [128, 512], F32, p3)
            ps_b = ps("ps_b3", [128, 512], F32, p3)
            nc.gpsimd.wait_ge(sem('lnapp'), NLNC0 * KT)
            nc.gpsimd.wait_ge(sem('statdn'), NLNC0)
            nc.gpsimd.wait_ge(sem('latev'), NKC * 2)
            nc.gpsimd.wait_ge(sem('krev'), NKC)
            wload(s_wq[:].rearrange("p (kt n) -> p kt n", kt=KT),
                  wq.rearrange("(kt p) n -> p kt n", p=128))
            wload(s_wqr[:].rearrange("p (kt n) -> p kt n", kt=KT),
                  wqr.rearrange("(kt p) n -> p kt n", p=128))
            W_QQR = wld_cnt[0]
            nc.sync.wait_ge(sem('lnapp'), NLNC0 * KT)
            nc.sync.wait_ge(sem('statdn'), NLNC0)
            nc.sync.wait_ge(sem('latev'), NKC * 2)
            nc.sync.wait_ge(sem('krev'), NKC)
            nc.sync.dma_start(out=s_cqf[:], in_=cosq).then_inc(sem('qtbl'), 16)
            nc.sync.dma_start(out=s_sqf[:], in_=sinq).then_inc(sem('qtbl'), 16)
            NLNC = NKC + NQC
            with nc.Fori(0, NQC) as c:
                nc.tensor.wait_ge(sem('wld'), W_QQR)
                nc.tensor.wait_ge(sem('lnapp'), NLNC * KT)
                nc.tensor.wait_ge(sem('statdn'), NLNC)
                nc.tensor.wait_ge(sem('latev'), NKC * 2)
                nc.tensor.wait_ge(sem('krev'), NKC)
                for mt in range(KQ):
                    pst = ps_a if (mt & 1) == 0 else ps_b
                    gq = c * KQ + mt
                    nc.tensor.wait_ge(sem('qcev'), smax(gq - 1, 0))
                    nc.tensor.wait_ge(sem('qrev'), c * NRT)
                    for kt in range(KT):
                        i2 = nc.tensor.matmul(
                            pst[:, 0:512],
                            s_wq[:, (kt * KQ + mt) * 128:(kt * KQ + mt) * 128 + 128],
                            s_lnq[:, ds(kt * TQ + c * 512, 512)],
                            start=(kt == 0), stop=(kt == KT - 1))
                        if kt == KT - 1:
                            i2.then_inc(sem('qcmm'), 1)
                    nc.vector.wait_ge(sem('qcmm'), gq + 1)
                    nc.vector.tensor_copy(s_qc[:, ds(mt * TQ + c * 512, 512)],
                                          pst[:, 0:512]).then_inc(sem('qcev'), 1)
                for mt in range(NRT):
                    pst = ps_a if (mt & 1) == 0 else ps_b
                    gq = c * NRT + mt
                    nc.tensor.wait_ge(sem('qrev'), smax(gq - 1, 0))
                    nc.tensor.wait_ge(sem('qcev'), (c + 1) * KQ)
                    for kt in range(KT):
                        i2 = nc.tensor.matmul(
                            pst[:, 0:512],
                            s_wqr[:, kt * (H * DR) + mt * 128:kt * (H * DR) + mt * 128 + 128],
                            s_lnq[:, ds(kt * TQ + c * 512, 512)],
                            start=(kt == 0), stop=(kt == KT - 1))
                        if kt == KT - 1:
                            i2.then_inc(sem('qrmm'), 1)
                    nc.vector.wait_ge(sem('qrmm'), gq + 1)
                    nc.vector.wait_ge(sem('qtbl'), 32)
                    nc.vector.drain()
                    qs = s_qrt
                    nc.vector.tensor_copy(qs[:, 0:512], pst[:, 0:512])
                    nc.vector.drain()
                    nc.vector.stream_shuffle(qs[:, 512:1024], qs[:, 0:512], SHUF)
                    nc.vector.drain()
                    nc.vector.tensor_mul(qs[:, 0:512], qs[:, 0:512],
                                         s_cqf[:, ds(c * 512, 512)])
                    nc.vector.tensor_mul(qs[:, 512:1024], qs[:, 512:1024],
                                         s_sqf[:, ds(c * 512, 512)])
                    nc.vector.drain()
                    nc.vector.tensor_add(s_qr[:, ds(mt * TQ + c * 512, 512)],
                                         qs[:, 0:512], qs[:, 512:1024]) \
                        .then_inc(sem('qrev'), 1)
            p3.__exit__(None, None, None)

        if phase < 4:
            nc.gpsimd.wait_ge(sem('qrev'), NQC * NRT)
            nc.gpsimd.wait_ge(sem('qcev'), NQC * KQ)
            nc.gpsimd.wait_ge(sem('wld'), wld_cnt[0])
            nc.gpsimd.dma_start(
                out=outT[:, 0:KQ * TQ // KT].rearrange("(kt p) n -> p kt n", p=128),
                in_=s_qc[:].rearrange("p (kt n) -> p kt n", kt=KT)) \
                .then_inc(sem('ostg'), 16)
            nc.gpsimd.wait_ge(sem('ostg'), 16)
            close_act()
            nc.compile()
            return nc
        with contextlib.ExitStack() as p4:
            s_wku = sb("s_wku", [128, 2 * HD], BF16, p4)
            s_wvu = sb("s_wvu", [128, 2 * HD], BF16, p4)
            s_kt0 = sb("s_kt0", [128, TK], BF16, p4)
            s_kt1 = sb("s_kt1", [128, TK], BF16, p4)
            s_qt0 = sb("s_qt0", [128, TQ], BF16, p4)
            s_qt1 = sb("s_qt1", [128, TQ], BF16, p4)
            s_vsb = sb("s_vsb", [128, 2 * NTT * 128], BF16, p4)
            s_wkst = sb("s_wkst", [128, 128], BF16, p4)
            s_wvst = sb("s_wvst", [128, 256], BF16, p4)
            s_lst = sb("s_lst", [128, 256], BF16, p4)
            s_qstg = sb("s_qstg", [128, 128], BF16, p4)
            s_zro = sb("s_zro", [128, 64], BF16, p4)
            s_rmx = sb("s_rmx", [128, 1], F32, p4)
            s_prob = sb("s_prob", [128, 2048], BF16, p4)
            s_ptst = sb("s_ptst", [128, 2048], BF16, p4)
            s_mst = sb("s_mst", [128, 16], F32, p4)
            s_fin = sb("s_fin", [128, 4], F32, p4)
            s_lob1 = sb("s_lob1", [128, 1], F32, p4)
            ps_sc = ps("ps_sc", [128, 2048], F32, p4)
            ps_pt0 = ps("ps_pt0", [128, 1024], BF16, p4)
            ps_pt1 = ps("ps_pt1", [128, 1024], BF16, p4)
            ps_y = ps("ps_y", [128, 128], F32, p4)
            ps_m1 = ps("ps_m1", [128, 512], F32, p4)

            nc.gpsimd.wait_ge(sem('qcmm'), NQC * KQ)
            nc.gpsimd.wait_ge(sem('qrmm'), NQC * NRT)
            nc.gpsimd.wait_ge(sem('qcev'), NQC * KQ)
            nc.gpsimd.wait_ge(sem('qrev'), NQC * NRT)
            wload(s_wku[:].rearrange("p (kt n) -> p kt n", kt=2),
                  wku.rearrange("(kt p) n -> p kt n", p=128))
            W_KU = wld_cnt[0]
            wload(s_wvu[:].rearrange("p (kt n) -> p kt n", kt=2),
                  wvu.rearrange("(kt p) n -> p kt n", p=128))
            W_VU = wld_cnt[0]
            nc.vector.wait_ge(sem('qcev'), NQC * KQ)
            nc.vector.wait_ge(sem('qrev'), NQC * NRT)
            nc.vector.memset(s_zro[:], 0.0)
            nc.vector.engine_nop().then_inc(sem('zro'), 1)
            nc.sync.wait_ge(sem('krev'), NKC)
            nc.sync.wait_ge(sem('qcmm'), NQC * KQ)
            nc.sync.wait_ge(sem('qrmm'), NQC * NRT)
            nc.sync.wait_ge(sem('qcev'), NQC * KQ)
            nc.sync.wait_ge(sem('qrev'), NQC * NRT)
            nc.sync.dma_start(out=s_kt0[64:96, :], in_=s_krf[0:32, :]).then_inc(sem('krcp'), 16)
            nc.sync.dma_start(out=s_kt1[64:96, :], in_=s_krf[0:32, :]).then_inc(sem('krcp'), 16)

            with nc.Fori(0, NHH) as hh:
                for u in range(4):
                    h01 = u % 2
                    vpar = u // 2
                    hg = hh * 4 + u
                    s_ktil = s_kt0 if h01 == 0 else s_kt1
                    s_qtil = s_qt0 if h01 == 0 else s_qt1
                    ps_ptP = (ps_pt0, ps_pt1)

                    if u in (0, 2):
                        hp = hh * 2 + vpar
                        nc.vector.wait_ge(sem('wld'), W_VU)
                        nc.vector.wait_ge(sem('rowd'), smax(2 * hp - 2, 0) * NQT)
                        nc.vector.wait_ge(sem('vmm'), hp * NTT)
                        nc.vector.tensor_copy(
                            s_wvst[:].rearrange("p (kt n) -> p kt n", kt=2),
                            bass.AP(s_wvu, hp * 128,
                                    [[2 * HD, 128], [HD, 2], [1, 128]]),
                        ).then_inc(sem('wvst'), 1)
                        nc.tensor.wait_ge(sem('wvst'), hp + 1)
                        nc.tensor.wait_ge(sem('latev'), NKC * 2)
                        nc.tensor.wait_ge(sem('kcev'), 2 * hp * NKC)
                        with nc.Fori(0, NTT) as tt:
                            gtt = hp * NTT + tt
                            nc.vector.wait_ge(sem('vmm'), gtt)
                            nc.vector.wait_ge(sem('latev'), NKC * 2)
                            nc.vector.tensor_copy(
                                s_lst[:].rearrange("p (kt n) -> p kt n", kt=2),
                                bass.AP(s_lat, tt * 128,
                                        [[2 * TK, 128], [TK, 2], [1, 128]]),
                            ).then_inc(sem('lstg'), 1)
                            nc.tensor.wait_ge(sem('lstg'), gtt + 1)
                            nc.tensor.wait_ge(sem('vev'), gtt)
                            nc.tensor.matmul(ps_m1[:, 0:128], s_lst[:, 0:128],
                                             s_wvst[:, 0:128], start=True, stop=False)
                            nc.tensor.matmul(ps_m1[:, 0:128], s_lst[:, 128:256],
                                             s_wvst[:, 128:256], start=False, stop=True) \
                                .then_inc(sem('vmm'), 1)
                            nc.vector.wait_ge(sem('vmm'), gtt + 1)
                            nc.vector.tensor_copy(
                                s_vsb[:, ds(vpar * NTT * 128 + tt * 128, 128)],
                                ps_m1[:, 0:128]).then_inc(sem('vev'), 1)

                    nc.vector.wait_ge(sem('wld'), W_KU)
                    nc.vector.wait_ge(sem('kcmm'), hg * NKC)
                    nc.vector.tensor_copy(
                        s_wkst[:].rearrange("p (kt n) -> p kt n", kt=2),
                        bass.AP(s_wku, (hh * 2 + vpar) * 128 + h01 * 64,
                                [[2 * HD, 128], [HD, 2], [1, 64]]),
                    ).then_inc(sem('wkst'), 1)
                    nc.tensor.wait_ge(sem('wkst'), hg + 1)
                    nc.tensor.wait_ge(sem('rowd'), smax(hg - 1, 0) * NQT)
                    nc.tensor.wait_ge(sem('krcp'), 32)
                    nc.tensor.wait_ge(sem('vev'), (hh * 2 + vpar + 1) * NTT)
                    with nc.Fori(0, NKC) as c8:
                        gkc = hg * NKC + c8
                        nc.tensor.wait_ge(sem('kcev'), gkc)
                        nc.tensor.matmul(ps_m1[0:64, 0:512], s_wkst[:, 0:64],
                                         s_lat[:, ds(c8 * 512, 512)],
                                         start=True, stop=False)
                        nc.tensor.matmul(ps_m1[0:64, 0:512], s_wkst[:, 64:128],
                                         s_lat[:, ds(TK + c8 * 512, 512)],
                                         start=False, stop=True).then_inc(sem('kcmm'), 1)
                        nc.vector.wait_ge(sem('kcmm'), gkc + 1)
                        nc.vector.tensor_copy(s_ktil[0:64, ds(c8 * 512, 512)],
                                              ps_m1[0:64, 0:512]).then_inc(sem('kcev'), 1)

                    nc.sync.wait_ge(sem('rowd'), smax(hg - 1, 0) * NQT)
                    nc.sync.wait_ge(sem('qcev'), NQC * KQ)
                    nc.sync.wait_ge(sem('qrev'), NQC * NRT)
                    nc.sync.wait_ge(sem('qasm'), hg * 32)
                    for hhc in nc.Switch(engines=[nc.sync],
                                         index=[nc.sync.compute_val(hh)], n=NHH):
                        nc.sync.dma_start(
                            out=s_qtil[0:64, :],
                            in_=s_qc[64 * h01:64 * h01 + 64,
                                     (hhc * 2 + vpar) * TQ:(hhc * 2 + vpar + 1) * TQ],
                        ).then_inc(sem('qasm'), 16)
                        nc.sync.dma_start(
                            out=s_qtil[64:96, :],
                            in_=s_qr[32 * u:32 * u + 32, hhc * TQ:(hhc + 1) * TQ],
                        ).then_inc(sem('qasm'), 16)
                    nc.vector.wait_ge(sem('cst'), 96)
                    nc.vector.wait_ge(sem('cxp'), hg * NQT)
                    nc.vector.tensor_copy(s_lob1[:], s_lobo[:, ds(hg, 1)]) \
                        .then_inc(sem('lobs'), 1)

                    with nc.Fori(0, NQT) as qt:
                        grow = hg * NQT + qt
                        k4 = qt >> 2
                        r4 = qt & 3
                        S = k4 + 1
                        NSP = (qt >> 3) + 1
                        gseg0 = hg * SPH + seg_before(qt)
                        nbr0 = hg * NB0H + nb0_before(qt)
                        nbr1 = gseg0 - nbr0
                        p0S = (S + 1) >> 1
                        p1S = S >> 1

                        nc.vector.wait_ge(sem('qasm'), (hg + 1) * 32)
                        nc.vector.wait_ge(sem('rd0'), nbr0)
                        nc.vector.wait_ge(sem('rd1'), nbr1)
                        nc.vector.tensor_copy(s_qstg[0:96, :],
                                              s_qtil[0:96, ds(qt * 128, 128)]) \
                            .then_inc(sem('qstg'), 1)
                        nc.vector.wait_ge(sem('ex0'), nbr0)
                        nc.vector.wait_ge(sem('ex1'), nbr1)
                        nc.vector.wait_ge(sem('cxp'), grow)
                        nc.vector.memset(s_mst[:, 0:8], NEG)
                        nc.vector.memset(s_mst[:, 8:16], 0.0)
                        nc.vector.engine_nop().then_inc(sem('mrst'), 1)
                        nc.tensor.wait_ge(sem('zro'), 1)
                        nc.tensor.wait_ge(sem('rowd'), grow)
                        nc.tensor.matmul(ps_y[:, 0:64], s_id[:, :], s_zro[:, 0:64],
                                         start=True, stop=True, skip_group_check=True)

                        with nc.Fori(0, NSP) as sp:
                            for par in (0, 1):
                                sidx = 2 * sp + par
                                o = (nbr0 + sp) if par == 0 else (nbr1 + sp)
                                g = gseg0 + sidx
                                eql = (sidx == k4)
                                vmm_sc = 4 * (g + 1) + 4 * grow + 4 * eql
                                pt = ps_ptP[par]
                                scb = ps_sc[:, par * 1024:par * 1024 + 1024]
                                prb = s_prob[:, par * 1024:par * 1024 + 1024]
                                ptb = s_ptst[:, par * 1024:par * 1024 + 1024]
                                sidx_c = smin(sidx, (NQT - 1) >> 2)

                                def pe_scores():
                                    nc.tensor.wait_ge(sem('qstg'), grow + 1)
                                    nc.tensor.wait_ge(sem('kcev'), (hg + 1) * NKC)
                                    nc.tensor.wait_ge(sem('ex%d' % par), o)
                                    for cch in range(4):
                                        nc.tensor.matmul(
                                            ps_sc[:, par * 1024 + cch * 256:
                                                  par * 1024 + cch * 256 + 256],
                                            s_qstg[0:96, :],
                                            s_ktil[0:96, ds(sidx_c * 1024 + cch * 256, 256)],
                                            start=(cch in (0, 2)), stop=True,
                                            skip_group_check=True) \
                                            .then_inc(sem('mmsc'), 1)

                                def pe_masks():
                                    for t in range(4):
                                        if t == 0:
                                            mofs = 0
                                        else:
                                            inr = (r4 + t <= 3)
                                            mofs = 1280 - inr * (1280 - 256 * t)
                                        nc.tensor.matmul(
                                            ps_sc[:, ds(par * 1024 +
                                                        256 * smin(r4 + t, 3), 256)],
                                            s_id[:, :],
                                            s_msk[:, ds(mofs, 256)],
                                            start=False, stop=True,
                                            skip_group_check=True) \
                                            .then_inc(sem('mmsc'), 1)

                                def pe_tail():
                                    nc.tensor.wait_ge(sem('ex%d' % par), o + 1)
                                    nc.tensor.wait_ge(sem('pv%d' % par), o)
                                    for lk in range(8):
                                        i2 = nc.tensor.transpose(
                                            pt[:, lk * 128:(lk + 1) * 128],
                                            s_prob[:, par * 1024 + lk * 128:
                                                   par * 1024 + lk * 128 + 128],
                                            s_id[:, :])
                                        if lk == 7:
                                            i2.then_inc(sem('tp%d' % par), 1)
                                    nc.tensor.wait_ge(sem('pv%d' % par), o + 1)
                                    nc.tensor.wait_ge(sem('vev'),
                                                      (hh * 2 + vpar + 1) * NTT)
                                    for lk in range(8):
                                        kgl = smin(sidx * 8 + lk, NTT - 1)
                                        vsl = s_vsb[:, ds(vpar * NTT * 128 +
                                                          kgl * 128 + h01 * 64, 64)]
                                        ptt = ptb[:, lk * 128:(lk + 1) * 128]
                                        i2 = nc.tensor.matmul(
                                            ps_y[:, 0:64], ptt, vsl,
                                            start=False, stop=True,
                                            skip_group_check=True)
                                        if lk == 7:
                                            i2.then_inc(sem('av%d' % par), 1)

                                def dve_block():
                                    nc.vector.wait_ge(sem('mmsc'), vmm_sc)
                                    nc.vector.wait_ge(sem('mrst'), grow + 1)
                                    nc.vector.reduce_max(s_rmx[:], scb, axis=AX.X)
                                    nc.vector.drain()
                                    nc.vector.tensor_copy(s_mst[:, ds(sidx, 1)],
                                                          s_rmx[:]) \
                                        .then_inc(sem('rd%d' % par), 1)
                                    nc.vector.wait_ge(sem('tp%d' % par), o + 1)
                                    nc.vector.wait_ge(sem('av%d' % par), o)
                                    nc.vector.tensor_copy(ptb, pt[:, :]) \
                                        .then_inc(sem('pv%d' % par), 1)

                                def act_block():
                                    nc.scalar.wait_ge(sem('rd%d' % par), o + 1)
                                    nc.scalar.wait_ge(sem('tp%d' % par), o)
                                    nc.scalar.activation(
                                        prb, scb, AF.Exp, bias=0.0, scale=SCALE,
                                        accum_out=s_mst[:, ds(8 + sidx, 1)]) \
                                        .then_inc(sem('ex%d' % par), 1)

                                vcond = 1 if par == 0 else (sidx <= k4)
                                mcond = (sidx == k4)
                                if par == 0:
                                    pe_scores()
                                else:
                                    for bi in nc.Switch(
                                            engines=[nc.tensor],
                                            index=[nc.tensor.compute_val(vcond)], n=2):
                                        if bi == 1:
                                            pe_scores()
                                for bi in nc.Switch(engines=[nc.tensor],
                                                    index=[nc.tensor.compute_val(mcond)],
                                                    n=2):
                                    if bi == 1:
                                        pe_masks()
                                if par == 0:
                                    pe_tail()
                                    dve_block()
                                    act_block()
                                else:
                                    for bi in nc.Switch(
                                            engines=[nc.tensor],
                                            index=[nc.tensor.compute_val(vcond)], n=2):
                                        if bi == 1:
                                            pe_tail()
                                    for bi in nc.Switch(
                                            engines=[nc.vector],
                                            index=[nc.vector.compute_val(vcond)], n=2):
                                        if bi == 1:
                                            dve_block()
                                    for bi in nc.Switch(
                                            engines=[nc.scalar],
                                            index=[nc.scalar.compute_val(vcond)], n=2):
                                        if bi == 1:
                                            act_block()

                        nc.vector.wait_ge(sem('ex0'), nbr0 + p0S)
                        nc.vector.wait_ge(sem('ex1'), nbr1 + p1S)
                        nc.vector.drain()
                        nc.vector.reduce_max(s_fin[:, 0:1], s_mst[:, 0:8], axis=AX.X) \
                            .then_inc(sem('fm'), 1)
                        nc.scalar.wait_ge(sem('fm'), grow + 1)
                        nc.scalar.wait_ge(sem('lobs'), hg + 1)
                        nc.scalar.wait_ge(sem('fd'), grow)
                        nc.scalar.activation(s_fin[:, 3:4], s_fin[:, 0:1], AF.Exp,
                                             bias=s_lob1[:], scale=SCALE) \
                            .then_inc(sem('cxp'), 1)
                        nc.vector.wait_ge(sem('cxp'), grow + 1)
                        nc.vector.reduce_sum(s_fin[:, 1:2], s_mst[:, 8:16], axis=AX.X)
                        nc.vector.drain()
                        nc.vector.tensor_add(s_fin[:, 1:2], s_fin[:, 1:2], s_fin[:, 3:4])
                        nc.vector.drain()
                        nc.vector.reciprocal(s_fin[:, 2:3], s_fin[:, 1:2])
                        nc.vector.drain()
                        nc.vector.engine_nop().then_inc(sem('fd'), 1)
                        nc.vector.wait_ge(sem('av0'), nbr0 + p0S)
                        nc.vector.wait_ge(sem('av1'), nbr1 + p1S)
                        nc.vector.tensor_scalar_mul(
                            s_yall[:, ds((qt * H + hg) * 64, 64)],
                            ps_y[:, 0:64], s_fin[:, 2:3]).then_inc(sem('rowd'), 1)

        if phase < 5:
            nc.gpsimd.wait_ge(sem('rowd'), H * NQT)
            nc.gpsimd.wait_ge(sem('wld'), wld_cnt[0])
            nc.gpsimd.dma_start(
                out=outT[:, 0:NQT * HD // KT].rearrange("(kt p) n -> p kt n", p=128),
                in_=s_yall[:].rearrange("p (kt n) -> p kt n", kt=KT)) \
                .then_inc(sem('ostg'), 16)
            nc.gpsimd.wait_ge(sem('ostg'), 16)
            close_act()
            nc.compile()
            return nc
        close_act()
        with contextlib.ExitStack() as p6:
            s_yfm = sb("s_yfm", [128, KQ * TQ], BF16, p6)
            s_wo = sb("s_wo", [128, KQ * E], BF16, p6)
            s_yst = sb("s_yst", [128, HD], BF16, p6)
            s_wst = sb("s_wst", [128, KQ * 128], BF16, p6)
            s_zc = sb("s_zc", [128, KT * 512], F32, p6)
            s_zsq = sb("s_zsq", [128, 512], F32, p6)
            s_zcm = sb("s_zcm", [128, 512], F32, p6)
            s_x2c = sb("s_x2c", [128, KT * 512], F32, p6)
            s_srow6 = sb("s_srow6", [1, 1536], F32, p6)
            ps_a6 = ps("ps_a6", [128, 512], F32, p6)
            ps_b6 = ps("ps_b6", [128, 512], F32, p6)
            ps_t6 = ps("ps_t6", [128, 1024], BF16, p6)
            ps_st6 = ps("ps_st6", [1, 1024], F32, p6)
            ps_pq6 = ps("ps_pq6", [128, 1024], F32, p6)

            nc.gpsimd.wait_ge(sem('rowd'), H * NQT)
            wload(s_wo[:].rearrange("p (kt n) -> p kt n", kt=KQ),
                  wo.rearrange("(kt p) n -> p kt n", p=128))
            W_O = wld_cnt[0]

            nc.vector.wait_ge(sem('rowd'), H * NQT)
            with nc.Fori(0, NQT) as qt:
                nc.vector.wait_ge(sem('ytr'), qt)
                nc.vector.tensor_copy(s_yst[:], s_yall[:, ds(qt * HD, HD)]) \
                    .then_inc(sem('ycp'), 1)
                nc.tensor.wait_ge(sem('ycp'), qt + 1)
                nc.tensor.wait_ge(sem('yev'), qt)
                for ft in range(KQ):
                    i2 = nc.tensor.transpose(ps_t6[:, ft * 128:(ft + 1) * 128],
                                             s_yst[:, ft * 128:(ft + 1) * 128],
                                             s_id[:, :])
                    if ft == KQ - 1:
                        i2.then_inc(sem('ytr'), 1)
                nc.vector.wait_ge(sem('ytr'), qt + 1)
                nc.vector.tensor_copy(
                    bass.AP(s_yfm, qt * 128, [[KQ * TQ, 128], [TQ, KQ], [1, 128]]),
                    ps_t6[:, 0:KQ * 128]).then_inc(sem('yev'), 1)

            if phase >= 6:
                with nc.Fori(0, NQC) as c:
                    nc.sync.wait_ge(sem('rowd'), H * NQT)
                    nc.sync.wait_ge(sem('x2st'), c * 16)
                    nc.sync.dma_start(
                        out=s_x2c[:].rearrange("p (kt n) -> p kt n", kt=KT),
                        in_=xoT[:, ds(c * 512, 512)].rearrange("(kt p) n -> p kt n", p=128),
                    ).then_inc(sem('xrl'), 16)
                    with nc.Fori(0, KT) as mt:
                        gcm = c * KT + mt
                        nc.vector.wait_ge(sem('wld'), W_O)
                        nc.vector.wait_ge(sem('womm'), gcm)
                        nc.vector.tensor_copy(
                            s_wst[:].rearrange("p (kt n) -> p kt n", kt=KQ),
                            bass.AP(s_wo, mt * 128, [[KQ * E, 128], [E, KQ], [1, 128]]),
                        ).then_inc(sem('wost'), 1)
                        nc.tensor.wait_ge(sem('wost'), gcm + 1)
                        nc.tensor.wait_ge(sem('yev'), NQT)
                        nc.tensor.wait_ge(sem('zev'), gcm)
                        nc.tensor.wait_ge(sem('zsq2'), gcm)
                        for kt in range(KQ):
                            i2 = nc.tensor.matmul(ps_a6[:, 0:512],
                                                  s_wst[:, kt * 128:(kt + 1) * 128],
                                                  s_yfm[:, ds(kt * TQ + c * 512, 512)],
                                                  start=(kt == 0), stop=(kt == KQ - 1))
                            if kt == KQ - 1:
                                i2.then_inc(sem('womm'), 1)
                        nc.vector.wait_ge(sem('womm'), gcm + 1)
                        nc.vector.tensor_copy(s_zc[:, ds(mt * 512, 512)], ps_a6[:, 0:512]) \
                            .then_inc(sem('zev'), 1)
                        nc.scalar.wait_ge(sem('zev'), gcm + 1)
                        nc.scalar.wait_ge(sem('zsq'), gcm)
                        nc.scalar.activation(s_zsq[:], s_zc[:, ds(mt * 512, 512)],
                                             AF.Square,
                                             bias=0.0, scale=1.0).then_inc(sem('zsq2'), 1)
                        if True:
                            nc.tensor.wait_ge(sem('zsq2'), gcm + 1)
                            nc.tensor.wait_ge(sem('zev'), gcm + 1)
                            nc.tensor.wait_ge(sem('ones'), 1)
                            for bi in nc.Switch(engines=[nc.tensor],
                                                index=[nc.tensor.compute_val(mt == 0)],
                                                n=2):
                                st_ = (bi == 1)
                                nc.tensor.matmul(ps_st6[0:1, 0:512], s_onesf[:, 0:1],
                                                 s_zc[:, ds(mt * 512, 512)],
                                                 start=st_, stop=True,
                                                 skip_group_check=True)
                                nc.tensor.matmul(ps_st6[0:1, 512:1024], s_onesf[:, 0:1],
                                                 s_zsq[:], start=st_, stop=True,
                                                 skip_group_check=True) \
                                    .then_inc(sem('zsq'), 1)
                    if True:
                        emit_ln_math('lnxB', c, s_srow6, ps_st6, ps_pq6,
                                     [('zsq', (c + 1) * KT)], [('x2ap', c * KT)])
                        nc.vector.wait_ge(sem('lnxB'), 4 * c + 4)
                        nc.vector.wait_ge(sem('xrl'), (c + 1) * 16)
                        nc.vector.wait_ge(sem('zev'), (c + 1) * KT)
                        for kt in range(KT):
                            nc.vector.drain()
                            nc.vector.scalar_tensor_tensor(
                                s_zcm[:], s_zc[:, ds(kt * 512, 512)], 1.0, ps_pq6[:, 0:512],
                                ALU.bypass, ALU.mult)
                            nc.vector.tensor_sub(s_x2c[:, ds(kt * 512, 512)],
                                                 s_x2c[:, ds(kt * 512, 512)], ps_pq6[:, 512:1024])
                            nc.vector.drain()
                            nc.vector.tensor_add(s_x2c[:, ds(kt * 512, 512)],
                                                 s_x2c[:, ds(kt * 512, 512)], s_zcm[:]) \
                                .then_inc(sem('x2ap'), 1)
                        nc.sync.wait_ge(sem('x2ap'), (c + 1) * KT)
                        nc.sync.dma_start(
                            out=x2d[:, ds(c * 512, 512)].rearrange("(kt p) n -> p kt n", p=128),
                            in_=s_x2c[:].rearrange("p (kt n) -> p kt n", kt=KT),
                        ).then_inc(sem('x2st'), 16)

        if phase < 7:
            if phase >= 6:
                nc.gpsimd.wait_ge(sem('x2st'), NQC * 16)
            else:
                nc.gpsimd.wait_ge(sem('yev'), NQT)
            nc.gpsimd.dma_start(out=outT[:, 0:TQ].rearrange("(kt p) n -> p kt n", p=128),
                                in_=x2d[:, 0:TQ].rearrange("(kt p) n -> p kt n", p=128)) \
                .then_inc(sem('ostg'), 16)
            nc.gpsimd.wait_ge(sem('ostg'), 16)
            nc.compile()
            return nc
        with contextlib.ExitStack() as p7:
            s_u16 = sb("s_u16", [128, KT * TQH], BF16, p7)
            s_fa = sb("s_fa", [128, KHID * TQH], BF16, p7)
            s_fb = sb("s_fb", [128, KHID * TQH], BF16, p7)
            s_w12 = sb("s_w12", [128, 4 * KT * 128], BF16, p7)
            s_hc = sb("s_hc", [128, 2 * KHID * 128], BF16, p7)
            s_x7 = sb("s_x7", [128, KT * 512], F32, p7)
            s_sq7 = sb("s_sq7", [128, 512], F32, p7)
            s_silu = sb("s_silu", [128, 512], BF16, p7)
            s_zsq7 = sb("s_zsq7", [128, 512], BF16, p7)
            s_fcm = sb("s_fcm", [128, 512], F32, p7)
            s_srow7 = sb("s_srow7", [1, 1536], F32, p7)
            ps_a7 = ps("ps_a7", [128, 512], F32, p7)
            ps_b7 = ps("ps_b7", [128, 512], F32, p7)
            ps_st7 = ps("ps_st7", [1, 1024], F32, p7)
            ps_pq7 = ps("ps_pq7", [128, 1024], F32, p7)

            NMT12 = SCLN // 2
            NMTH = KHID // 2
            NMT3 = KT // 2

            with nc.Fori(0, 2) as half:
                tof = half * TQH

                with nc.Fori(0, NCH) as c:
                    cc7 = half * NCH + c
                    nc.sync.wait_ge(sem('x2st'), NQC * 16)
                    nc.sync.wait_ge(sem('l2ap'), cc7 * KT)
                    nc.sync.wait_ge(sem('oap'), half * NCH * KT)
                    nc.sync.wait_ge(sem('ost'), half * NCH * 16)
                    nc.sync.dma_start(
                        out=s_x7[:].rearrange("p (kt n) -> p kt n", kt=KT),
                        in_=x2d[:, ds(tof + c * 512, 512)].rearrange("(kt p) n -> p kt n", p=128),
                    ).then_inc(sem('xl7'), 16)
                    with nc.Fori(0, KT) as ktl:
                        gl = cc7 * KT + ktl
                        nc.scalar.wait_ge(sem('xl7'), (cc7 + 1) * 16)
                        nc.scalar.wait_ge(sem('st7'), gl)
                        nc.scalar.activation(s_sq7[:], s_x7[:, ds(ktl * 512, 512)],
                                             AF.Square, bias=0.0, scale=1.0) \
                            .then_inc(sem('xq7'), 1)
                        nc.tensor.wait_ge(sem('xl7'), (cc7 + 1) * 16)
                        nc.tensor.wait_ge(sem('xq7'), gl + 1)
                        nc.tensor.wait_ge(sem('st7d'), cc7)
                        nc.tensor.wait_ge(sem('fsd'), half * NCH)
                        for bi in nc.Switch(engines=[nc.tensor],
                                            index=[nc.tensor.compute_val(ktl == 0)], n=2):
                            st_ = (bi == 1)
                            nc.tensor.matmul(ps_st7[0:1, 0:512], s_onesf[:, 0:1],
                                             s_x7[:, ds(ktl * 512, 512)],
                                             start=st_, stop=True, skip_group_check=True)
                            nc.tensor.matmul(ps_st7[0:1, 512:1024], s_onesf[:, 0:1],
                                             s_sq7[:], start=st_, stop=True,
                                             skip_group_check=True).then_inc(sem('st7'), 1)
                    emit_ln_math('lnxC', cc7, s_srow7, ps_st7, ps_pq7,
                                 [('st7', (cc7 + 1) * KT)],
                                 [('l2ap', cc7 * KT), ('fsd', half * NCH)])
                    nc.vector.engine_nop().then_inc(sem('st7d'), 1)
                    nc.vector.wait_ge(sem('lnxC'), 4 * cc7 + 4)
                    for kt in range(KT):
                        nc.vector.scalar_tensor_tensor(
                            s_fcm[:], s_x7[:, ds(kt * 512, 512)], 1.0,
                            ps_pq7[:, 0:512], ALU.bypass, ALU.mult)
                        nc.vector.drain()
                        nc.vector.tensor_sub(s_u16[:, ds(kt * TQH + c * 512, 512)],
                                             s_fcm[:], ps_pq7[:, 512:1024]) \
                            .then_inc(sem('l2ap'), 1)
                        nc.vector.drain()

                def w12dma(wsrc, mt_expr, slot, ordv):
                    nc.gpsimd.wait_ge(sem('f12%d' % slot), ordv * 16)
                    nc.gpsimd.dma_start(
                        out=s_w12[:, slot * 1024:(slot + 1) * 1024]
                        .rearrange("p (kt n) -> p kt n", kt=KT),
                        in_=wsrc[:, ds(mt_expr * 128, 128)].rearrange("(kt p) n -> p kt n", p=128),
                    ).then_inc(sem('f12%d' % slot), 16)

                nc.gpsimd.wait_ge(sem('x2st'), NQC * 16)
                for wpar in (0, 1):
                    w12dma(w1t, wpar, wpar, half * NMT12)
                    w12dma(w2t, wpar, wpar + 2, half * NMT12)
                with nc.Fori(0, NMT12) as mtp:
                    for wpar in (0, 1):
                        mt = 2 * mtp + wpar
                        gm = half * SCLN + mt
                        nc.vector.wait_ge(sem('cst'), 96)
                        nc.vector.wait_ge(sem('f3m'), half * KT * NCH)
                        nc.vector.wait_ge(sem('sil'), smax((gm - 1) * NCH + 1, 0))
                        nc.vector.tensor_copy(s_stg[:, 1 + wpar:2 + wpar],
                                              s_scl[:, ds(mt, 1)]) \
                            .then_inc(sem('sstg'), 1)
                        for nch in range(NCH):
                            gmn = gm * NCH + nch
                            nc.tensor.wait_ge(sem('f12%d' % wpar),
                                              (half * NMT12 + mtp + 1) * 16)
                            nc.tensor.wait_ge(sem('f12%d' % (wpar + 2)),
                                              (half * NMT12 + mtp + 1) * 16)
                            nc.tensor.wait_ge(sem('l2ap'), (half + 1) * NCH * KT)
                            nc.tensor.wait_ge(sem('sil'), gmn)
                            nc.tensor.wait_ge(sem('gate'), gmn)
                            for kt in range(KT):
                                i2 = nc.tensor.matmul(
                                    ps_a7[:, 0:512],
                                    s_w12[:, wpar * 1024 + kt * 128:wpar * 1024 + kt * 128 + 128],
                                    s_u16[:, ds(kt * TQH + nch * 512, 512)],
                                    start=(kt == 0), stop=(kt == KT - 1))
                            for kt in range(KT):
                                i2 = nc.tensor.matmul(
                                    ps_b7[:, 0:512],
                                    s_w12[:, (wpar + 2) * 1024 + kt * 128:
                                          (wpar + 2) * 1024 + kt * 128 + 128],
                                    s_u16[:, ds(kt * TQH + nch * 512, 512)],
                                    start=(kt == 0), stop=(kt == KT - 1))
                                if kt == KT - 1:
                                    i2.then_inc(sem('w12mm'), 1)
                            nc.scalar.wait_ge(sem('w12mm'), gmn + 1)
                            nc.scalar.wait_ge(sem('sstg'), gm + 1)
                            nc.scalar.wait_ge(sem('gate'), gmn)
                            nc.scalar.activation(s_silu[:], ps_a7[:, 0:512], AF.Silu,
                                                 bias=0.0,
                                                 scale=s_stg[:, 1 + wpar:2 + wpar]) \
                                .then_inc(sem('sil'), 1)
                            nc.vector.wait_ge(sem('sil'), gmn + 1)
                            nc.vector.scalar_tensor_tensor(
                                s_fa[:, ds(mt * TQH + nch * 512, 512)],
                                ps_b7[:, 0:512], s_scl[:, ds(SCLN + mt, 1)],
                                s_silu[:], ALU.mult, ALU.mult).then_inc(sem('gate'), 1)
                        with nc.gpsimd.If_lt(nc.gpsimd.to_reg(mt), SCLN - 2):
                            nc.gpsimd.wait_ge(sem('w12mm'), (gm + 1) * NCH)
                            w12dma(w1t, smin(mt + 2, SCLN - 1), wpar,
                                   half * NMT12 + mtp + 1)
                            w12dma(w2t, smin(mt + 2, SCLN - 1), wpar + 2,
                                   half * NMT12 + mtp + 1)

                def hstage(wsrc, src_sb, dst_sb, stgcol, sclbase, phname,
                           prev_sem, prev_val):
                    phm = sem(phname + 'm')
                    phl0 = sem(phname + 'l0')
                    phl1 = sem(phname + 'l1')
                    phs = sem(phname + 's')
                    phe = sem(phname + 'e')
                    phls = (phl0, phl1)

                    def hdma(mt_expr, slot, ordv):
                        nc.gpsimd.wait_ge(phls[slot], ordv * 16)
                        nc.gpsimd.dma_start(
                            out=s_hc[:, slot * KHID * 128:(slot + 1) * KHID * 128]
                            .rearrange("p (kt n) -> p kt n", kt=KHID),
                            in_=wsrc[:, ds(mt_expr * 128, 128)]
                            .rearrange("(kt p) n -> p kt n", p=128),
                        ).then_inc(phls[slot], 16)

                    for wpar in (0, 1):
                        nc.gpsimd.wait_ge(prev_sem, prev_val)
                        hdma(wpar, wpar, half * NMTH)
                    with nc.Fori(0, NMTH) as mtp:
                        for wpar in (0, 1):
                            mt = 2 * mtp + wpar
                            gm = half * KHID + mt
                            nc.vector.wait_ge(sem('cst'), 96)
                            nc.vector.wait_ge(phe, smax((gm - 1) * NCH, 0))
                            nc.vector.tensor_copy(s_stg[:, stgcol + wpar:stgcol + wpar + 1],
                                                  s_scl[:, ds(sclbase + mt, 1)]) \
                                .then_inc(phs, 1)
                            for nch in range(NCH):
                                gmn = gm * NCH + nch
                                nc.tensor.wait_ge(phls[wpar],
                                                  (half * NMTH + mtp + 1) * 16)
                                nc.tensor.wait_ge(phe, smax(gmn - 1, 0))
                                pst = ps_a7 if (nch & 1) == 0 else ps_b7
                                for kt in range(KHID):
                                    i2 = nc.tensor.matmul(
                                        pst[:, 0:512],
                                        s_hc[:, wpar * KHID * 128 + kt * 128:
                                              wpar * KHID * 128 + kt * 128 + 128],
                                        src_sb[:, ds(kt * TQH + nch * 512, 512)],
                                        start=(kt == 0), stop=(kt == KHID - 1))
                                    if kt == KHID - 1:
                                        i2.then_inc(phm, 1)
                                nc.scalar.wait_ge(phm, gmn + 1)
                                nc.scalar.wait_ge(phs, gm + 1)
                                nc.scalar.activation(
                                    dst_sb[:, ds(mt * TQH + nch * 512, 512)],
                                    pst[:, 0:512], AF.Gelu,
                                    bias=0.0,
                                    scale=s_stg[:, stgcol + wpar:stgcol + wpar + 1]) \
                                    .then_inc(phe, 1)
                            with nc.gpsimd.If_lt(nc.gpsimd.to_reg(mt), KHID - 2):
                                nc.gpsimd.wait_ge(phm, (gm + 1) * NCH)
                                hdma(smin(mt + 2, KHID - 1), wpar,
                                     half * NMTH + mtp + 1)

                nc.tensor.wait_ge(sem('gate'), (half + 1) * SCLN * NCH)
                hstage(h1t, s_fa, s_fb, 3, 2 * SCLN, 'h1', sem('f3m'), half * KT * NCH)
                nc.tensor.wait_ge(sem('h1e'), (half + 1) * KHID * NCH)
                hstage(h2t, s_fb, s_fa, 5, 3 * SCLN, 'h2', sem('h1m'), (half + 1) * KHID * NCH)

                nc.tensor.wait_ge(sem('h2e'), (half + 1) * KHID * NCH)
                def w3dma(mt_expr, slot, ordv):
                    sname = 'f3l%d' % slot
                    nc.gpsimd.wait_ge(sem(sname), ordv * 16)
                    nc.gpsimd.dma_start(
                        out=s_hc[:, slot * KHID * 128:(slot + 1) * KHID * 128]
                        .rearrange("p (kt n) -> p kt n", kt=KHID),
                        in_=w3t[:, ds(mt_expr * 128, 128)].rearrange("(kt p) n -> p kt n", p=128),
                    ).then_inc(sem(sname), 16)

                for wpar in (0, 1):
                    nc.gpsimd.wait_ge(sem('h2m'), (half + 1) * KHID * NCH)
                    w3dma(wpar, wpar, half * NMT3)
                with nc.Fori(0, NMT3) as mtp:
                    for wpar in (0, 1):
                        mt = 2 * mtp + wpar
                        gm = half * KT + mt
                        for nch in range(NCH):
                            gmn = gm * NCH + nch
                            nc.tensor.wait_ge(sem('f3l%d' % wpar),
                                              (half * NMT3 + mtp + 1) * 16)
                            nc.tensor.wait_ge(sem('f3e'), smax(gmn - 1, 0))
                            pst = ps_a7 if (nch & 1) == 0 else ps_b7
                            for kt in range(KHID):
                                i2 = nc.tensor.matmul(
                                    pst[:, 0:512],
                                    s_hc[:, wpar * KHID * 128 + kt * 128:
                                          wpar * KHID * 128 + kt * 128 + 128],
                                    s_fa[:, ds(kt * TQH + nch * 512, 512)],
                                    start=(kt == 0), stop=(kt == KHID - 1))
                                if kt == KHID - 1:
                                    i2.then_inc(sem('f3m'), 1)
                            nc.vector.wait_ge(sem('f3m'), gmn + 1)
                            nc.vector.wait_ge(sem('oap'), half * NCH * KT)
                            nc.vector.tensor_scalar_mul(
                                s_u16[:, ds(mt * TQH + nch * 512, 512)],
                                pst[:, 0:512], s_scl[:, ds(4 * SCLN + mt, 1)]) \
                                .then_inc(sem('f3e'), 1)
                        with nc.gpsimd.If_lt(nc.gpsimd.to_reg(mt), KT - 2):
                            nc.gpsimd.wait_ge(sem('f3m'), (gm + 1) * NCH)
                            w3dma(smin(mt + 2, KT - 1), wpar,
                                  half * NMT3 + mtp + 1)

                with nc.Fori(0, NCH) as c:
                    cf = half * NCH + c
                    nc.sync.wait_ge(sem('oap'), cf * KT)
                    nc.sync.wait_ge(sem('ost'), cf * 16)
                    nc.sync.wait_ge(sem('l2ap'), (half + 1) * NCH * KT)
                    nc.sync.dma_start(
                        out=s_x7[:].rearrange("p (kt n) -> p kt n", kt=KT),
                        in_=x2d[:, ds(tof + c * 512, 512)].rearrange("(kt p) n -> p kt n", p=128),
                    ).then_inc(sem('xf7'), 16)
                    nc.tensor.wait_ge(sem('f3e'), (half + 1) * KT * NCH)
                    nc.tensor.wait_ge(sem('fsd'), cf)
                    nc.tensor.wait_ge(sem('st7d'), (half + 1) * NCH)
                    with nc.Fori(0, KT) as kt7:
                        gz = cf * KT + kt7
                        nc.scalar.wait_ge(sem('fst'), gz)
                        nc.scalar.wait_ge(sem('f3e'), (half + 1) * KT * NCH)
                        nc.scalar.activation(s_zsq7[:],
                                             s_u16[:, ds(kt7 * TQH + c * 512, 512)],
                                             AF.Square, bias=0.0, scale=1.0) \
                            .then_inc(sem('fsq'), 1)
                        nc.tensor.wait_ge(sem('fsq'), gz + 1)
                        for bi in nc.Switch(engines=[nc.tensor],
                                            index=[nc.tensor.compute_val(kt7 == 0)], n=2):
                            st_ = (bi == 1)
                            nc.tensor.matmul(ps_st7[0:1, 0:512], s_ones[:, 0:1],
                                             s_u16[:, ds(kt7 * TQH + c * 512, 512)],
                                             start=st_, stop=True, skip_group_check=True)
                            nc.tensor.matmul(ps_st7[0:1, 512:1024], s_ones[:, 0:1],
                                             s_zsq7[:], start=st_, stop=True,
                                             skip_group_check=True).then_inc(sem('fst'), 1)

                    emit_ln_math('lnxD', cf, s_srow7, ps_st7, ps_pq7,
                                 [('fst', (cf + 1) * KT)],
                                 [('oap', cf * KT), ('l2ap', (half + 1) * NCH * KT)])
                    nc.vector.wait_ge(sem('lnxD'), 4 * cf + 4)
                    nc.vector.wait_ge(sem('xf7'), (cf + 1) * 16)
                    nc.vector.wait_ge(sem('f3e'), (half + 1) * KT * NCH)
                    for kt in range(KT):
                        nc.vector.drain()
                        nc.vector.scalar_tensor_tensor(
                            s_fcm[:], s_u16[:, ds(kt * TQH + c * 512, 512)], 1.0,
                            ps_pq7[:, 0:512], ALU.bypass, ALU.mult)
                        nc.vector.tensor_sub(s_x7[:, ds(kt * 512, 512)],
                                             s_x7[:, ds(kt * 512, 512)],
                                             ps_pq7[:, 512:1024])
                        nc.vector.drain()
                        nc.vector.tensor_add(s_x7[:, ds(kt * 512, 512)],
                                             s_x7[:, ds(kt * 512, 512)], s_fcm[:]) \
                            .then_inc(sem('oap'), 1)
                    nc.vector.engine_nop().then_inc(sem('fsd'), 1)
                    nc.sync.wait_ge(sem('oap'), (cf + 1) * KT)
                    nc.sync.dma_start(
                        out=outT[:, ds(tof + c * 512, 512)].rearrange("(kt p) n -> p kt n", p=128),
                        in_=s_x7[:].rearrange("p (kt n) -> p kt n", kt=KT),
                    ).then_inc(sem('ost'), 16)

            nc.sync.wait_ge(sem('ost'), NCH * 16)
            nc.gpsimd.wait_ge(sem('ost'), NCH * 16)

    nc.compile()
    return nc



def host_prepare(cfg: Cfg, inputs, batch, parity):
    TK, TQ, H, DHID = cfg.TK, cfg.TQ, cfg.H, cfg.DHID
    NQT = cfg.NQT
    x = np.asarray(inputs["L0_attn_x"])[batch]
    pos = np.concatenate([np.arange(128 * (2 * j + parity), 128 * (2 * j + parity) + 128)
                          for j in range(NQT)])
    xf = np.ascontiguousarray(x[:TK].T.astype(np.float32))
    xo = np.ascontiguousarray(x[pos].T.astype(np.float32))

    inv = 1.0 / 10000.0 ** (np.arange(16, dtype=np.float32) / 16.0)

    def qtabs(positions):
        ang = positions[:, None].astype(np.float32) * inv[None, :]
        c = np.cos(ang).T
        s = np.sin(ang).T
        cos128 = np.zeros((128, len(positions)), np.float32)
        sin128 = np.zeros((128, len(positions)), np.float32)
        for r in range(128):
            d = r % 32
            cos128[r] = c[d % 16]
            sin128[r] = -s[d] if d < 16 else s[d - 16]
        return cos128, sin128

    cq, sq = qtabs(pos)
    angk = np.arange(TK, dtype=np.float32)[:, None] * inv[None, :]
    ck = np.cos(angk).T
    skt = np.sin(angk).T
    cosk = np.zeros((32, TK), np.float32)
    sink = np.zeros((32, TK), np.float32)
    for d in range(32):
        cosk[d] = ck[d % 16]
        sink[d] = -skt[d] if d < 16 else skt[d - 16]

    mt = np.zeros((128, 1536), np.float32)
    tril = np.where(np.arange(128)[:, None] >= np.arange(128)[None, :], 0.0, NEG)
    if parity == 0:
        mt[:, 0:128] = tril
        mt[:, 128:256] = NEG
    else:
        mt[:, 0:128] = 0.0
        mt[:, 128:256] = tril
    mt[:, 256:1280] = NEG

    tile_scl = lambda v, n: np.ascontiguousarray(
        np.asarray(v, np.float32).reshape(n, 128).T)

    d = {
        "xfT": xf, "xoT": xo,
        "wq": np.ascontiguousarray(np.asarray(inputs["L0_attn_Wq"], np.float32)),
        "wqr": np.ascontiguousarray(np.asarray(inputs["L0_attn_Wqr"], np.float32)),
        "wkr": np.ascontiguousarray(np.asarray(inputs["L0_attn_Wkr"], np.float32)),
        "wkvd": np.ascontiguousarray(np.asarray(inputs["L0_attn_Wkvd"], np.float32)),
        "wku": np.ascontiguousarray(np.asarray(inputs["L0_attn_Wku"], np.float32)),
        "wvu": np.ascontiguousarray(np.asarray(inputs["L0_attn_Wvu"], np.float32)),
        "wo": np.ascontiguousarray(np.asarray(inputs["L0_attn_Wo"], np.float32)),
        "w1t": np.ascontiguousarray(np.asarray(inputs["L0_ffn_w1_w"], np.float32).T),
        "w2t": np.ascontiguousarray(np.asarray(inputs["L0_ffn_w2_w"], np.float32).T),
        "h1t": np.ascontiguousarray(np.asarray(inputs["L0_ffn_h1_w"], np.float32).T),
        "h2t": np.ascontiguousarray(np.asarray(inputs["L0_ffn_h2_w"], np.float32).T),
        "w3t": np.ascontiguousarray(np.asarray(inputs["L0_ffn_w3_w"], np.float32).T),
        "s1s": tile_scl(inputs["L0_ffn_w1_s"], DHID // 128),
        "s2s": tile_scl(inputs["L0_ffn_w2_s"], DHID // 128),
        "h1ss": tile_scl(inputs["L0_ffn_h1_s"], DHID // 128),
        "h2ss": tile_scl(inputs["L0_ffn_h2_s"], DHID // 128),
        "s3s": tile_scl(inputs["L0_ffn_w3_s"], KT),
        "lobo": np.ascontiguousarray(np.broadcast_to(
            np.asarray(inputs["L0_attn_lobo_log"], np.float32)[None, :], (128, H)).copy()),
        "ident": np.eye(128, dtype=np.float32),
        "masktab": mt,
        "cosq": cq, "sinq": sq, "cosk": cosk, "sink": sink,
    }
    return d, pos


_NC_CACHE = {}


def kernel(**inputs):
    cfg = Cfg()
    key = (cfg.TK, cfg.TQ, cfg.H, cfg.DHID)
    if key not in _NC_CACHE:
        _NC_CACHE[key] = build_nc(cfg)
    nc = _NC_CACHE[key]
    in_maps = []
    poss = []
    for core in range(8):
        d, pos = host_prepare(cfg, inputs, core // 2, core % 2)
        in_maps.append(d)
        poss.append(pos)
    res = run_bass_kernel_spmd(nc, in_maps, core_ids=list(range(8)))
    out = np.zeros((4, 4096, 1024), dtype=np.float32)
    for core in range(8):
        out[core // 2][poss[core]] = res.results[core]["outT"].T
    return out


# version 05 (final)
# speedup vs baseline: 1.3006x; 1.3006x over previous
import contextlib
import os as _os
import numpy as np
import ml_dtypes

import concourse.bass as bass
import concourse.bacc as bacc
import concourse.mybir as mybir
from concourse.bass_utils import run_bass_kernel_spmd
from concourse.expressions_rust import smin, smax

F32 = mybir.dt.float32
BF16 = mybir.dt.bfloat16
AF = mybir.ActivationFunctionType
AX = mybir.AxisListType
ALU = mybir.AluOpType
ds = bass.ds

SCALE = float(1.0 / np.sqrt(96.0))
EPS = 1e-5
NEG = -1.0e30
E = 1024
KT = 8
DKV = 256
DR = 32
SHUF = [(i + 16) % 32 for i in range(32)]


class Cfg:
    def __init__(self, TK=4096, TQ=2048, H=16, DHID=4096):
        self.TK, self.TQ, self.H, self.DHID = TK, TQ, H, DHID
        self.NQT = TQ // 128
        self.NKC = TK // 512
        self.NQC = TQ // 512
        self.NTT = TK // 128
        self.KHID = DHID // 128
        self.NHH = H // 4
        self.TQH = TQ // 2
        self.NCH = self.TQH // 512
        assert TQ % 1024 == 0 and TK % 512 == 0 and H % 4 == 0
        self.SPH = sum(j // 4 + 1 for j in range(self.NQT))
        self.NB0H = sum((j // 4 + 2) // 2 for j in range(self.NQT))


def seg_before(qt):
    k = qt >> 2
    r = qt & 3
    return qt + 2 * k * k - 2 * k + r * k


def nb0_before(qt):
    m = qt >> 3
    r = qt & 7
    return qt + 4 * m * m - 4 * m + r * m


def build_nc(cfg: Cfg, phase=7):
    P6SUB = int(_os.environ.get('P6SUB', '9'))
    nc = bacc.Bacc("TRN2", target_bir_lowering=False)
    TK, TQ, H, DHID, KHID = cfg.TK, cfg.TQ, cfg.H, cfg.DHID, cfg.KHID
    NQT, NKC, NQC, NTT, NHH = cfg.NQT, cfg.NKC, cfg.NQC, cfg.NTT, cfg.NHH
    NCH, TQH = cfg.NCH, cfg.TQH
    HD = H * 64
    KQ = HD // 128
    NRT = H // 4
    SCLN = DHID // 128
    SPH, NB0H = cfg.SPH, cfg.NB0H

    din = lambda n, shp: nc.dram_tensor(n, shp, F32, kind="ExternalInput").ap()
    xfT = din("xfT", [E, TK])
    xoT = din("xoT", [E, TQ])
    wq = din("wq", [E, HD])
    wqr = din("wqr", [E, H * DR])
    wkr = din("wkr", [E, DR])
    wkvd = din("wkvd", [E, DKV])
    wku = din("wku", [DKV, HD])
    wvu = din("wvu", [DKV, HD])
    wo = din("wo", [HD, E])
    w1t = din("w1t", [E, DHID])
    w2t = din("w2t", [E, DHID])
    h1t = din("h1t", [DHID, DHID])
    h2t = din("h2t", [DHID, DHID])
    w3t = din("w3t", [DHID, E])
    s1s = din("s1s", [128, SCLN])
    s2s = din("s2s", [128, SCLN])
    h1ss = din("h1ss", [128, SCLN])
    h2ss = din("h2ss", [128, SCLN])
    s3s = din("s3s", [128, KT])
    lobo = din("lobo", [128, H])
    ident = din("ident", [128, 128])
    masktab = din("masktab", [128, 1536])
    cosq = din("cosq", [128, TQ])
    sinq = din("sinq", [128, TQ])
    cosk = din("cosk", [32, TK])
    sink = din("sink", [32, TK])
    outT = nc.dram_tensor("outT", [E, TQ], F32, kind="ExternalOutput").ap()
    x2d = nc.dram_tensor("x2d", [E, TQ], F32, kind="Internal").ap()

    ctx = contextlib.ExitStack()
    sems = {}

    def sem(name):
        if name not in sems:
            sems[name] = ctx.enter_context(nc.semaphore(name))
        return sems[name]

    def sb(n, shp, dt=BF16, st=None):
        return (st or ctx).enter_context(nc.sbuf_tensor(n, shp, dt))

    def ps(n, shp, dt=F32, st=None):
        return (st or ctx).enter_context(nc.psum_tensor(n, shp, dt))

    wld_cnt = [0]

    def wload(dst3d, src3d):
        nc.gpsimd.wait_ge(sem('wld'), wld_cnt[0])
        nc.gpsimd.dma_start(out=dst3d, in_=src3d).then_inc(sem('wld'), 16)
        wld_cnt[0] += 16
        return wld_cnt[0]

    def emit_ln_math(lnx, cc, srow, ps_stat_, ps_pq_, dve_waits, pe_waits):
        for s_, v_ in dve_waits:
            nc.vector.wait_ge(sem(s_), v_)
        nc.vector.tensor_scalar_mul(srow[0:1, 0:512], ps_stat_[0:1, 0:512], 1.0 / E)
        nc.vector.tensor_scalar_mul(srow[0:1, 512:1024], ps_stat_[0:1, 512:1024], 1.0 / E)
        nc.vector.drain()
        nc.vector.tensor_mul(srow[0:1, 1024:1536], srow[0:1, 0:512], srow[0:1, 0:512])
        nc.vector.drain()
        nc.vector.tensor_sub(srow[0:1, 512:1024], srow[0:1, 512:1024],
                             srow[0:1, 1024:1536])
        nc.vector.drain()
        nc.vector.tensor_scalar_add(srow[0:1, 512:1024], srow[0:1, 512:1024],
                                    EPS).then_inc(sem(lnx), 1)
        nc.scalar.wait_ge(sem(lnx), 4 * cc + 1)
        nc.scalar.activation(srow[0:1, 1024:1536], srow[0:1, 512:1024],
                             AF.Sqrt, bias=0.0, scale=1.0).then_inc(sem(lnx), 1)
        nc.vector.wait_ge(sem(lnx), 4 * cc + 2)
        nc.vector.reciprocal(srow[0:1, 512:1024], srow[0:1, 1024:1536])
        nc.vector.drain()
        nc.vector.tensor_mul(srow[0:1, 0:512], srow[0:1, 0:512],
                             srow[0:1, 512:1024])
        nc.vector.engine_nop().then_inc(sem(lnx), 1)
        nc.tensor.wait_ge(sem(lnx), 4 * cc + 3)
        for s_, v_ in pe_waits:
            nc.tensor.wait_ge(sem(s_), v_)
        nc.tensor.matmul(ps_pq_[:, 0:512], s_one1f[0:1, :], srow[0:1, 512:1024],
                         start=True, stop=True)
        nc.tensor.matmul(ps_pq_[:, 512:1024], s_one1f[0:1, :], srow[0:1, 0:512],
                         start=True, stop=True).then_inc(sem(lnx), 1)

    def emit_ln_chunk(cc, xst_v, xsq_v, lnout, ln_col, ln_stride,
                      ps_stat, ps_pq, s_srow, s_ones, s_one1):
        nc.scalar.wait_ge(sem('xld'), (cc + 1) * 16)
        nc.scalar.wait_ge(sem('lnapp'), cc * KT)
        nc.scalar.activation(xsq_v, xst_v, AF.Square, bias=0.0, scale=1.0) \
            .then_inc(sem('xsq'), 1)
        nc.tensor.wait_ge(sem('ones'), 1)
        nc.tensor.wait_ge(sem('xld'), (cc + 1) * 16)
        nc.tensor.wait_ge(sem('xsq'), cc + 1)
        nc.tensor.wait_ge(sem('statdn'), cc)
        for kt in range(KT):
            nc.tensor.matmul(ps_stat[0:1, 0:512], s_ones[:, 0:1],
                             xst_v[:, kt * 512:(kt + 1) * 512],
                             start=(kt == 0), stop=(kt == KT - 1))
            i2 = nc.tensor.matmul(ps_stat[0:1, 512:1024], s_ones[:, 0:1],
                                  xsq_v[:, kt * 512:(kt + 1) * 512],
                                  start=(kt == 0), stop=(kt == KT - 1))
            if kt == KT - 1:
                i2.then_inc(sem('statmm'), 1)
        nc.vector.engine_nop().then_inc(sem('statdn'), 1) if False else None
        emit_ln_math('lnxA', cc, s_srow, ps_stat, ps_pq,
                     [('statmm', cc + 1)], [('lnapp', cc * KT)])
        nc.vector.engine_nop().then_inc(sem('statdn'), 1)
        nc.vector.wait_ge(sem('lnxA'), 4 * cc + 4)
        for kt in range(KT):
            nc.vector.tensor_mul(xsq_v[:, kt * 512:(kt + 1) * 512],
                                 xst_v[:, kt * 512:(kt + 1) * 512], ps_pq[:, 0:512])
            nc.vector.drain()
            nc.vector.tensor_sub(lnout[:, ds(kt * ln_stride + ln_col, 512)],
                                 xsq_v[:, kt * 512:(kt + 1) * 512],
                                 ps_pq[:, 512:1024]).then_inc(sem('lnapp'), 1)
            nc.vector.drain()

    with ctx:
        s_id = sb("s_id", [128, 128])
        s_msk = sb("s_msk", [128, 1536])
        s_lobo = sb("s_lobo", [128, H], F32)
        s_scl = sb("s_scl", [128, 4 * SCLN + KT], F32)
        s_ones = sb("s_ones", [128, 1])
        s_one1 = sb("s_one1", [1, 128])
        s_onesf = sb("s_onesf", [128, 1], F32)
        s_one1f = sb("s_one1f", [1, 128], F32)
        s_stg = sb("s_stg", [128, 8], F32)

        nc.sync.dma_start(out=s_lobo[:], in_=lobo).then_inc(sem('cst'), 16)
        nc.sync.dma_start(out=s_scl[:, 0:SCLN], in_=s1s).then_inc(sem('cst'), 16)
        nc.sync.dma_start(out=s_scl[:, SCLN:2 * SCLN], in_=s2s).then_inc(sem('cst'), 16)
        nc.sync.dma_start(out=s_scl[:, 2 * SCLN:3 * SCLN], in_=h1ss).then_inc(sem('cst'), 16)
        nc.sync.dma_start(out=s_scl[:, 3 * SCLN:4 * SCLN], in_=h2ss).then_inc(sem('cst'), 16)
        nc.sync.dma_start(out=s_scl[:, 4 * SCLN:4 * SCLN + KT], in_=s3s).then_inc(sem('cst'), 16)
        wload(s_id[:], ident)
        wload(s_msk[:], masktab)
        nc.vector.memset(s_ones[:], 1.0)
        nc.vector.memset(s_one1[0:1, :], 1.0)
        nc.vector.memset(s_onesf[:], 1.0)
        nc.vector.memset(s_one1f[0:1, :], 1.0)
        nc.vector.engine_nop().then_inc(sem('ones'), 1)

        sact = contextlib.ExitStack()
        sact.__enter__()
        s_lat = sb("s_lat", [128, 2 * TK], BF16, sact)
        s_krf = sb("s_krf", [32, TK], BF16, sact)
        s_qc = sb("s_qc", [128, KQ * TQ], BF16, sact)
        s_qr = sb("s_qr", [128, NRT * TQ], BF16, sact)
        s_lnq = sb("s_lnq", [128, KT * TQ], BF16, sact)
        syall = contextlib.ExitStack()
        syall.__enter__()
        s_yall = syall.enter_context(
            nc.sbuf_tensor("s_yall", [128, NQT * HD], BF16, side="right"))

        def close_act():
            syall.__exit__(None, None, None)
            sact.__exit__(None, None, None)

        if True:
            p1 = contextlib.ExitStack()
            p1.__enter__()
            s_ckc = sb("s_ckc", [32, 2048], F32, p1)
            s_xst = sb("s_xst", [128, 2 * 4096], BF16, p1)
            s_xsq = sb("s_xsq", [128, 4096], BF16, p1)
            s_lnc = sb("s_lnc", [128, 2 * 4096], BF16, p1)
            s_srow = sb("s_srow", [1, 1536], F32, p1)
            s_wkvd = sb("s_wkvd", [128, KT * DKV], BF16, p1)
            s_wkr = sb("s_wkr", [128, KT * DR], BF16, p1)
            s_krt = sb("s_krt", [32, 1024], F32, p1)
            ps_stat = ps("ps_stat", [1, 1024], F32, p1)
            ps_pq = ps("ps_pq", [128, 1024], F32, p1)
            ps_a = ps("ps_a", [128, 512], F32, p1)
            ps_b = ps("ps_b", [128, 512], F32, p1)
            ps_kr = ps("ps_kr", [32, 512], F32, p1)

            wload(s_wkvd[:].rearrange("p (kt n) -> p kt n", kt=KT),
                  wkvd.rearrange("(kt p) n -> p kt n", p=128))
            wload(s_wkr[:].rearrange("p (kt n) -> p kt n", kt=KT),
                  wkr.rearrange("(kt p) n -> p kt n", p=128))
            W_KVKR = wld_cnt[0]
            nc.sync.dma_start(out=s_ckc[0:32, 0:512], in_=cosk[:, 0:512]) \
                .then_inc(sem('tbl'), 16)
            nc.sync.dma_start(out=s_ckc[0:32, 512:1024], in_=sink[:, 0:512]) \
                .then_inc(sem('tbl'), 16)

            with nc.Fori(0, NKC) as c:
                par = c & 1
                nc.gpsimd.wait_ge(sem('xld'), c * 16)
                nc.gpsimd.wait_ge(sem('lnapp'), smax(c - 1, 0) * KT)
                nc.gpsimd.dma_start(
                    out=s_xst[:, ds(par * 4096, 4096)].rearrange("p (kt n) -> p kt n", kt=KT),
                    in_=xfT[:, ds(c * 512, 512)].rearrange("(kt p) n -> p kt n", p=128),
                ).then_inc(sem('xld'), 16)
                with nc.sync.If_lt(nc.sync.to_reg(c), NKC - 1):
                    nc.sync.wait_ge(sem('krev'), c)
                    nc.sync.wait_ge(sem('tbl'), (c + 1) * 32)
                    cnx = smin(c + 1, NKC - 1)
                    nc.sync.dma_start(out=s_ckc[0:32, ds((1 - par) * 1024, 512)],
                                      in_=cosk[:, ds(cnx * 512, 512)]) \
                        .then_inc(sem('tbl'), 16)
                    nc.sync.dma_start(out=s_ckc[0:32, ds((1 - par) * 1024 + 512, 512)],
                                      in_=sink[:, ds(cnx * 512, 512)]) \
                        .then_inc(sem('tbl'), 16)
                xst_v = s_xst[:, ds(par * 4096, 4096)]
                xsq_v = s_xsq[:]
                lnc_v = s_lnc[:, ds(par * 4096, 4096)]
                emit_ln_chunk(c, xst_v, xsq_v, s_lnc, par * 4096, 512,
                              ps_stat, ps_pq, s_srow, s_ones, s_one1)
                nc.tensor.wait_ge(sem('wld'), W_KVKR)
                nc.tensor.wait_ge(sem('lnapp'), (c + 1) * KT)
                nc.tensor.wait_ge(sem('latev'), c * 2)
                for mt in range(2):
                    pst = ps_a if mt == 0 else ps_b
                    for kt in range(KT):
                        i2 = nc.tensor.matmul(
                            pst[:, 0:512],
                            s_wkvd[:, (kt * 2 + mt) * 128:(kt * 2 + mt) * 128 + 128],
                            lnc_v[:, kt * 512:(kt + 1) * 512],
                            start=(kt == 0), stop=(kt == KT - 1))
                        if kt == KT - 1:
                            i2.then_inc(sem('latmm'), 1)
                for mt in range(2):
                    pst = ps_a if mt == 0 else ps_b
                    nc.vector.wait_ge(sem('latmm'), c * 2 + mt + 1)
                    nc.vector.tensor_copy(s_lat[:, ds(mt * TK + c * 512, 512)],
                                          pst[:, 0:512]).then_inc(sem('latev'), 1)
                nc.tensor.wait_ge(sem('krev'), c)
                for kt in range(KT):
                    i2 = nc.tensor.matmul(ps_kr[0:32, 0:512],
                                          s_wkr[:, kt * DR:kt * DR + 32],
                                          lnc_v[:, kt * 512:(kt + 1) * 512],
                                          start=(kt == 0), stop=(kt == KT - 1))
                    if kt == KT - 1:
                        i2.then_inc(sem('krmm'), 1)
                nc.vector.wait_ge(sem('krmm'), c + 1)
                nc.vector.wait_ge(sem('tbl'), (c + 1) * 32)
                nc.vector.tensor_copy(s_krt[0:32, 0:512], ps_kr[0:32, 0:512])
                nc.vector.drain()
                nc.vector.stream_shuffle(s_krt[0:32, 512:1024], s_krt[0:32, 0:512], SHUF)
                nc.vector.drain()
                nc.vector.tensor_mul(s_krt[0:32, 0:512], s_krt[0:32, 0:512],
                                     s_ckc[0:32, ds(par * 1024, 512)])
                nc.vector.tensor_mul(s_krt[0:32, 512:1024], s_krt[0:32, 512:1024],
                                     s_ckc[0:32, ds(par * 1024 + 512, 512)])
                nc.vector.drain()
                nc.vector.tensor_add(s_krf[0:32, ds(c * 512, 512)],
                                     s_krt[0:32, 0:512], s_krt[0:32, 512:1024]) \
                    .then_inc(sem('krev'), 1)

            with nc.Fori(0, NQC) as c:
                cc = NKC + c
                par = cc & 1
                nc.gpsimd.wait_ge(sem('xld'), cc * 16)
                nc.gpsimd.wait_ge(sem('lnapp'), smax(cc - 1, 0) * KT)
                nc.gpsimd.dma_start(
                    out=s_xst[:, ds(par * 4096, 4096)].rearrange("p (kt n) -> p kt n", kt=KT),
                    in_=xoT[:, ds(c * 512, 512)].rearrange("(kt p) n -> p kt n", p=128),
                ).then_inc(sem('xld'), 16)
                xst_v = s_xst[:, ds(par * 4096, 4096)]
                xsq_v = s_xsq[:]
                emit_ln_chunk(cc, xst_v, xsq_v, s_lnq, c * 512, TQ,
                              ps_stat, ps_pq, s_srow, s_ones, s_one1)

            if phase < 3:
                nc.gpsimd.wait_ge(sem('lnapp'), (NKC + NQC) * KT)
                nc.gpsimd.wait_ge(sem('xld'), (NKC + NQC) * 16)
                nc.gpsimd.wait_ge(sem('latev'), NKC * 2)
                nc.gpsimd.wait_ge(sem('krev'), NKC)
                nc.gpsimd.dma_start(out=outT[:, 0:512].rearrange("(kt p) n -> p kt n", p=128),
                                    in_=s_xst[:, 0:4096].rearrange("p (kt n) -> p kt n", kt=KT)) \
                    .then_inc(sem('ostg'), 16)
                nc.gpsimd.wait_ge(sem('ostg'), 16)
                p1.__exit__(None, None, None)
                close_act()
                nc.compile()
                return nc
            p1.__exit__(None, None, None)
            p3 = contextlib.ExitStack()
            p3.__enter__()
            NLNC0 = NKC + NQC
            s_wq = sb("s_wq", [128, KT * HD], BF16, p3)
            s_wqr = sb("s_wqr", [128, KT * H * DR], BF16, p3)
            s_cqf = sb("s_cqf", [128, TQ], F32, p3)
            s_sqf = sb("s_sqf", [128, TQ], F32, p3)
            s_qrt = sb("s_qrt", [128, 1024], F32, p3)
            ps_a = ps("ps_a3", [128, 512], F32, p3)
            ps_b = ps("ps_b3", [128, 512], F32, p3)
            nc.gpsimd.wait_ge(sem('lnapp'), NLNC0 * KT)
            nc.gpsimd.wait_ge(sem('statdn'), NLNC0)
            nc.gpsimd.wait_ge(sem('latev'), NKC * 2)
            nc.gpsimd.wait_ge(sem('krev'), NKC)
            wload(s_wq[:].rearrange("p (kt n) -> p kt n", kt=KT),
                  wq.rearrange("(kt p) n -> p kt n", p=128))
            wload(s_wqr[:].rearrange("p (kt n) -> p kt n", kt=KT),
                  wqr.rearrange("(kt p) n -> p kt n", p=128))
            W_QQR = wld_cnt[0]
            nc.sync.wait_ge(sem('lnapp'), NLNC0 * KT)
            nc.sync.wait_ge(sem('statdn'), NLNC0)
            nc.sync.wait_ge(sem('latev'), NKC * 2)
            nc.sync.wait_ge(sem('krev'), NKC)
            nc.sync.dma_start(out=s_cqf[:], in_=cosq).then_inc(sem('qtbl'), 16)
            nc.sync.dma_start(out=s_sqf[:], in_=sinq).then_inc(sem('qtbl'), 16)
            NLNC = NKC + NQC
            with nc.Fori(0, NQC) as c:
                nc.tensor.wait_ge(sem('wld'), W_QQR)
                nc.tensor.wait_ge(sem('lnapp'), NLNC * KT)
                nc.tensor.wait_ge(sem('statdn'), NLNC)
                nc.tensor.wait_ge(sem('latev'), NKC * 2)
                nc.tensor.wait_ge(sem('krev'), NKC)
                for mt in range(KQ):
                    pst = ps_a if (mt & 1) == 0 else ps_b
                    gq = c * KQ + mt
                    nc.tensor.wait_ge(sem('qcev'), smax(gq - 1, 0))
                    nc.tensor.wait_ge(sem('qrev'), c * NRT)
                    for kt in range(KT):
                        i2 = nc.tensor.matmul(
                            pst[:, 0:512],
                            s_wq[:, (kt * KQ + mt) * 128:(kt * KQ + mt) * 128 + 128],
                            s_lnq[:, ds(kt * TQ + c * 512, 512)],
                            start=(kt == 0), stop=(kt == KT - 1))
                        if kt == KT - 1:
                            i2.then_inc(sem('qcmm'), 1)
                    nc.vector.wait_ge(sem('qcmm'), gq + 1)
                    nc.vector.tensor_copy(s_qc[:, ds(mt * TQ + c * 512, 512)],
                                          pst[:, 0:512]).then_inc(sem('qcev'), 1)
                for mt in range(NRT):
                    pst = ps_a if (mt & 1) == 0 else ps_b
                    gq = c * NRT + mt
                    nc.tensor.wait_ge(sem('qrev'), smax(gq - 1, 0))
                    nc.tensor.wait_ge(sem('qcev'), (c + 1) * KQ)
                    for kt in range(KT):
                        i2 = nc.tensor.matmul(
                            pst[:, 0:512],
                            s_wqr[:, kt * (H * DR) + mt * 128:kt * (H * DR) + mt * 128 + 128],
                            s_lnq[:, ds(kt * TQ + c * 512, 512)],
                            start=(kt == 0), stop=(kt == KT - 1))
                        if kt == KT - 1:
                            i2.then_inc(sem('qrmm'), 1)
                    nc.vector.wait_ge(sem('qrmm'), gq + 1)
                    nc.vector.wait_ge(sem('qtbl'), 32)
                    nc.vector.drain()
                    qs = s_qrt
                    nc.vector.tensor_copy(qs[:, 0:512], pst[:, 0:512])
                    nc.vector.drain()
                    nc.vector.stream_shuffle(qs[:, 512:1024], qs[:, 0:512], SHUF)
                    nc.vector.drain()
                    nc.vector.tensor_mul(qs[:, 0:512], qs[:, 0:512],
                                         s_cqf[:, ds(c * 512, 512)])
                    nc.vector.tensor_mul(qs[:, 512:1024], qs[:, 512:1024],
                                         s_sqf[:, ds(c * 512, 512)])
                    nc.vector.drain()
                    nc.vector.tensor_add(s_qr[:, ds(mt * TQ + c * 512, 512)],
                                         qs[:, 0:512], qs[:, 512:1024]) \
                        .then_inc(sem('qrev'), 1)
            p3.__exit__(None, None, None)

        if phase < 4:
            nc.gpsimd.wait_ge(sem('qrev'), NQC * NRT)
            nc.gpsimd.wait_ge(sem('qcev'), NQC * KQ)
            nc.gpsimd.wait_ge(sem('wld'), wld_cnt[0])
            nc.gpsimd.dma_start(
                out=outT[:, 0:KQ * TQ // KT].rearrange("(kt p) n -> p kt n", p=128),
                in_=s_qc[:].rearrange("p (kt n) -> p kt n", kt=KT)) \
                .then_inc(sem('ostg'), 16)
            nc.gpsimd.wait_ge(sem('ostg'), 16)
            close_act()
            nc.compile()
            return nc
        with contextlib.ExitStack() as p4:
            s_wku = sb("s_wku", [128, 2 * HD], BF16, p4)
            s_wvu = sb("s_wvu", [128, 2 * HD], BF16, p4)
            s_kt0 = sb("s_kt0", [128, TK], BF16, p4)
            s_kt1 = sb("s_kt1", [128, TK], BF16, p4)
            s_qt0 = sb("s_qt0", [128, TQ], BF16, p4)
            s_qt1 = sb("s_qt1", [128, TQ], BF16, p4)
            s_vsb = sb("s_vsb", [128, 2 * NTT * 128], BF16, p4)
            s_wkst = sb("s_wkst", [128, 128], BF16, p4)
            s_wvst = sb("s_wvst", [128, 256], BF16, p4)
            s_lst = sb("s_lst", [128, 256], BF16, p4)
            s_qstg = sb("s_qstg", [128, 128], BF16, p4)
            s_zro = sb("s_zro", [128, 64], BF16, p4)
            s_rmx = sb("s_rmx", [128, 1], F32, p4)
            s_prob = sb("s_prob", [128, 2048], BF16, p4)
            s_ptst = sb("s_ptst", [128, 2048], BF16, p4)
            s_mst = sb("s_mst", [128, 16], F32, p4)
            s_fin = sb("s_fin", [128, 4], F32, p4)
            s_lob1 = sb("s_lob1", [128, 1], F32, p4)
            ps_sc = ps("ps_sc", [128, 2048], F32, p4)
            ps_pt0 = ps("ps_pt0", [128, 1024], BF16, p4)
            ps_pt1 = ps("ps_pt1", [128, 1024], BF16, p4)
            ps_y = ps("ps_y", [128, 128], F32, p4)
            ps_m1 = ps("ps_m1", [128, 512], F32, p4)

            nc.gpsimd.wait_ge(sem('qcmm'), NQC * KQ)
            nc.gpsimd.wait_ge(sem('qrmm'), NQC * NRT)
            nc.gpsimd.wait_ge(sem('qcev'), NQC * KQ)
            nc.gpsimd.wait_ge(sem('qrev'), NQC * NRT)
            wload(s_wku[:].rearrange("p (kt n) -> p kt n", kt=2),
                  wku.rearrange("(kt p) n -> p kt n", p=128))
            W_KU = wld_cnt[0]
            wload(s_wvu[:].rearrange("p (kt n) -> p kt n", kt=2),
                  wvu.rearrange("(kt p) n -> p kt n", p=128))
            W_VU = wld_cnt[0]
            nc.vector.wait_ge(sem('qcev'), NQC * KQ)
            nc.vector.wait_ge(sem('qrev'), NQC * NRT)
            nc.vector.memset(s_zro[:], 0.0)
            nc.vector.engine_nop().then_inc(sem('zro'), 1)
            nc.sync.wait_ge(sem('krev'), NKC)
            nc.sync.wait_ge(sem('qcmm'), NQC * KQ)
            nc.sync.wait_ge(sem('qrmm'), NQC * NRT)
            nc.sync.wait_ge(sem('qcev'), NQC * KQ)
            nc.sync.wait_ge(sem('qrev'), NQC * NRT)
            nc.sync.dma_start(out=s_kt0[64:96, :], in_=s_krf[0:32, :]).then_inc(sem('krcp'), 16)
            nc.sync.dma_start(out=s_kt1[64:96, :], in_=s_krf[0:32, :]).then_inc(sem('krcp'), 16)

            with nc.Fori(0, NHH) as hh:
                for u in range(4):
                    h01 = u % 2
                    vpar = u // 2
                    hg = hh * 4 + u
                    s_ktil = s_kt0 if h01 == 0 else s_kt1
                    s_qtil = s_qt0 if h01 == 0 else s_qt1
                    ps_ptP = (ps_pt0, ps_pt1)

                    if u in (0, 2):
                        hp = hh * 2 + vpar
                        nc.vector.wait_ge(sem('wld'), W_VU)
                        nc.vector.wait_ge(sem('rowd'), smax(2 * hp - 2, 0) * NQT)
                        nc.vector.wait_ge(sem('vmm'), hp * NTT)
                        nc.vector.tensor_copy(
                            s_wvst[:].rearrange("p (kt n) -> p kt n", kt=2),
                            bass.AP(s_wvu, hp * 128,
                                    [[2 * HD, 128], [HD, 2], [1, 128]]),
                        ).then_inc(sem('wvst'), 1)
                        nc.tensor.wait_ge(sem('wvst'), hp + 1)
                        nc.tensor.wait_ge(sem('latev'), NKC * 2)
                        nc.tensor.wait_ge(sem('kcev'), 2 * hp * NKC)
                        with nc.Fori(0, NTT) as tt:
                            gtt = hp * NTT + tt
                            nc.vector.wait_ge(sem('vmm'), gtt)
                            nc.vector.wait_ge(sem('latev'), NKC * 2)
                            nc.vector.tensor_copy(
                                s_lst[:].rearrange("p (kt n) -> p kt n", kt=2),
                                bass.AP(s_lat, tt * 128,
                                        [[2 * TK, 128], [TK, 2], [1, 128]]),
                            ).then_inc(sem('lstg'), 1)
                            nc.tensor.wait_ge(sem('lstg'), gtt + 1)
                            nc.tensor.wait_ge(sem('vev'), gtt)
                            nc.tensor.matmul(ps_m1[:, 0:128], s_lst[:, 0:128],
                                             s_wvst[:, 0:128], start=True, stop=False)
                            nc.tensor.matmul(ps_m1[:, 0:128], s_lst[:, 128:256],
                                             s_wvst[:, 128:256], start=False, stop=True) \
                                .then_inc(sem('vmm'), 1)
                            nc.vector.wait_ge(sem('vmm'), gtt + 1)
                            nc.vector.tensor_copy(
                                s_vsb[:, ds(vpar * NTT * 128 + tt * 128, 128)],
                                ps_m1[:, 0:128]).then_inc(sem('vev'), 1)

                    nc.vector.wait_ge(sem('wld'), W_KU)
                    nc.vector.wait_ge(sem('kcmm'), hg * NKC)
                    nc.vector.tensor_copy(
                        s_wkst[:].rearrange("p (kt n) -> p kt n", kt=2),
                        bass.AP(s_wku, (hh * 2 + vpar) * 128 + h01 * 64,
                                [[2 * HD, 128], [HD, 2], [1, 64]]),
                    ).then_inc(sem('wkst'), 1)
                    nc.tensor.wait_ge(sem('wkst'), hg + 1)
                    nc.tensor.wait_ge(sem('rowd'), smax(hg - 1, 0) * NQT)
                    nc.tensor.wait_ge(sem('krcp'), 32)
                    nc.tensor.wait_ge(sem('vev'), (hh * 2 + vpar + 1) * NTT)
                    with nc.Fori(0, NKC) as c8:
                        gkc = hg * NKC + c8
                        nc.tensor.wait_ge(sem('kcev'), gkc)
                        nc.tensor.matmul(ps_m1[0:64, 0:512], s_wkst[:, 0:64],
                                         s_lat[:, ds(c8 * 512, 512)],
                                         start=True, stop=False)
                        nc.tensor.matmul(ps_m1[0:64, 0:512], s_wkst[:, 64:128],
                                         s_lat[:, ds(TK + c8 * 512, 512)],
                                         start=False, stop=True).then_inc(sem('kcmm'), 1)
                        nc.vector.wait_ge(sem('kcmm'), gkc + 1)
                        nc.vector.tensor_copy(s_ktil[0:64, ds(c8 * 512, 512)],
                                              ps_m1[0:64, 0:512]).then_inc(sem('kcev'), 1)

                    nc.sync.wait_ge(sem('rowd'), smax(hg - 1, 0) * NQT)
                    nc.sync.wait_ge(sem('qcev'), NQC * KQ)
                    nc.sync.wait_ge(sem('qrev'), NQC * NRT)
                    nc.sync.wait_ge(sem('qasm'), hg * 32)
                    for hhc in nc.Switch(engines=[nc.sync],
                                         index=[nc.sync.compute_val(hh)], n=NHH):
                        nc.sync.dma_start(
                            out=s_qtil[0:64, :],
                            in_=s_qc[64 * h01:64 * h01 + 64,
                                     (hhc * 2 + vpar) * TQ:(hhc * 2 + vpar + 1) * TQ],
                        ).then_inc(sem('qasm'), 16)
                        nc.sync.dma_start(
                            out=s_qtil[64:96, :],
                            in_=s_qr[32 * u:32 * u + 32, hhc * TQ:(hhc + 1) * TQ],
                        ).then_inc(sem('qasm'), 16)
                    nc.vector.wait_ge(sem('cst'), 96)
                    nc.vector.wait_ge(sem('cxp'), hg * NQT)
                    nc.vector.tensor_copy(s_lob1[:], s_lobo[:, ds(hg, 1)]) \
                        .then_inc(sem('lobs'), 1)

                    with nc.Fori(0, NQT) as qt:
                        grow = hg * NQT + qt
                        k4 = qt >> 2
                        r4 = qt & 3
                        S = k4 + 1
                        NSP = (qt >> 3) + 1
                        gseg0 = hg * SPH + seg_before(qt)
                        nbr0 = hg * NB0H + nb0_before(qt)
                        nbr1 = gseg0 - nbr0
                        p0S = (S + 1) >> 1
                        p1S = S >> 1

                        nc.vector.wait_ge(sem('qasm'), (hg + 1) * 32)
                        nc.vector.wait_ge(sem('rd0'), nbr0)
                        nc.vector.wait_ge(sem('rd1'), nbr1)
                        nc.vector.tensor_copy(s_qstg[0:96, :],
                                              s_qtil[0:96, ds(qt * 128, 128)]) \
                            .then_inc(sem('qstg'), 1)
                        nc.vector.wait_ge(sem('ex0'), nbr0)
                        nc.vector.wait_ge(sem('ex1'), nbr1)
                        nc.vector.wait_ge(sem('cxp'), grow)
                        nc.vector.memset(s_mst[:, 0:8], NEG)
                        nc.vector.memset(s_mst[:, 8:16], 0.0)
                        nc.vector.engine_nop().then_inc(sem('mrst'), 1)
                        nc.tensor.wait_ge(sem('zro'), 1)
                        nc.tensor.wait_ge(sem('rowd'), grow)
                        nc.tensor.matmul(ps_y[:, 0:64], s_id[:, :], s_zro[:, 0:64],
                                         start=True, stop=True, skip_group_check=True)

                        with nc.Fori(0, NSP) as sp:
                            for par in (0, 1):
                                sidx = 2 * sp + par
                                o = (nbr0 + sp) if par == 0 else (nbr1 + sp)
                                g = gseg0 + sidx
                                eql = (sidx == k4)
                                vmm_sc = 4 * (g + 1) + 4 * grow + 4 * eql
                                pt = ps_ptP[par]
                                scb = ps_sc[:, par * 1024:par * 1024 + 1024]
                                prb = s_prob[:, par * 1024:par * 1024 + 1024]
                                ptb = s_ptst[:, par * 1024:par * 1024 + 1024]
                                sidx_c = smin(sidx, (NQT - 1) >> 2)

                                def pe_scores():
                                    nc.tensor.wait_ge(sem('qstg'), grow + 1)
                                    nc.tensor.wait_ge(sem('kcev'), (hg + 1) * NKC)
                                    nc.tensor.wait_ge(sem('ex%d' % par), o)
                                    for cch in range(4):
                                        nc.tensor.matmul(
                                            ps_sc[:, par * 1024 + cch * 256:
                                                  par * 1024 + cch * 256 + 256],
                                            s_qstg[0:96, :],
                                            s_ktil[0:96, ds(sidx_c * 1024 + cch * 256, 256)],
                                            start=(cch in (0, 2)), stop=True,
                                            skip_group_check=True) \
                                            .then_inc(sem('mmsc'), 1)

                                def pe_masks():
                                    for t in range(4):
                                        if t == 0:
                                            mofs = 0
                                        else:
                                            inr = (r4 + t <= 3)
                                            mofs = 1280 - inr * (1280 - 256 * t)
                                        nc.tensor.matmul(
                                            ps_sc[:, ds(par * 1024 +
                                                        256 * smin(r4 + t, 3), 256)],
                                            s_id[:, :],
                                            s_msk[:, ds(mofs, 256)],
                                            start=False, stop=True,
                                            skip_group_check=True) \
                                            .then_inc(sem('mmsc'), 1)

                                def pe_tail():
                                    nc.tensor.wait_ge(sem('ex%d' % par), o + 1)
                                    nc.tensor.wait_ge(sem('pv%d' % par), o)
                                    for lk in range(8):
                                        i2 = nc.tensor.transpose(
                                            pt[:, lk * 128:(lk + 1) * 128],
                                            s_prob[:, par * 1024 + lk * 128:
                                                   par * 1024 + lk * 128 + 128],
                                            s_id[:, :])
                                        if lk == 7:
                                            i2.then_inc(sem('tp%d' % par), 1)
                                    nc.tensor.wait_ge(sem('pv%d' % par), o + 1)
                                    nc.tensor.wait_ge(sem('vev'),
                                                      (hh * 2 + vpar + 1) * NTT)
                                    for lk in range(8):
                                        kgl = smin(sidx * 8 + lk, NTT - 1)
                                        vsl = s_vsb[:, ds(vpar * NTT * 128 +
                                                          kgl * 128 + h01 * 64, 64)]
                                        ptt = ptb[:, lk * 128:(lk + 1) * 128]
                                        i2 = nc.tensor.matmul(
                                            ps_y[:, 0:64], ptt, vsl,
                                            start=False, stop=True,
                                            skip_group_check=True)
                                        if lk == 7:
                                            i2.then_inc(sem('av%d' % par), 1)

                                def dve_block():
                                    nc.vector.wait_ge(sem('mmsc'), vmm_sc)
                                    nc.vector.wait_ge(sem('mrst'), grow + 1)
                                    nc.vector.reduce_max(s_rmx[:], scb, axis=AX.X)
                                    nc.vector.drain()
                                    nc.vector.tensor_copy(s_mst[:, ds(sidx, 1)],
                                                          s_rmx[:]) \
                                        .then_inc(sem('rd%d' % par), 1)
                                    nc.vector.wait_ge(sem('tp%d' % par), o + 1)
                                    nc.vector.wait_ge(sem('av%d' % par), o)
                                    nc.vector.tensor_copy(ptb, pt[:, :]) \
                                        .then_inc(sem('pv%d' % par), 1)

                                def act_block():
                                    nc.scalar.wait_ge(sem('rd%d' % par), o + 1)
                                    nc.scalar.wait_ge(sem('tp%d' % par), o)
                                    nc.scalar.activation(
                                        prb, scb, AF.Exp, bias=0.0, scale=SCALE,
                                        accum_out=s_mst[:, ds(8 + sidx, 1)]) \
                                        .then_inc(sem('ex%d' % par), 1)

                                vcond = 1 if par == 0 else (sidx <= k4)
                                mcond = (sidx == k4)
                                pe_idx = (1 + mcond) if par == 0 else (vcond + mcond)
                                for bi in nc.Switch(engines=[nc.tensor],
                                                    index=[nc.tensor.compute_val(pe_idx)],
                                                    n=3):
                                    if bi >= 1:
                                        pe_scores()
                                        if bi == 2:
                                            pe_masks()
                                        pe_tail()
                                if par == 0:
                                    dve_block()
                                    act_block()
                                else:
                                    for bi in nc.Switch(
                                            engines=[nc.vector],
                                            index=[nc.vector.compute_val(vcond)], n=2):
                                        if bi == 1:
                                            dve_block()
                                    for bi in nc.Switch(
                                            engines=[nc.scalar],
                                            index=[nc.scalar.compute_val(vcond)], n=2):
                                        if bi == 1:
                                            act_block()

                        nc.vector.wait_ge(sem('ex0'), nbr0 + p0S)
                        nc.vector.wait_ge(sem('ex1'), nbr1 + p1S)
                        nc.vector.drain()
                        nc.vector.reduce_max(s_fin[:, 0:1], s_mst[:, 0:8], axis=AX.X) \
                            .then_inc(sem('fm'), 1)
                        nc.scalar.wait_ge(sem('fm'), grow + 1)
                        nc.scalar.wait_ge(sem('lobs'), hg + 1)
                        nc.scalar.wait_ge(sem('fd'), grow)
                        nc.scalar.activation(s_fin[:, 3:4], s_fin[:, 0:1], AF.Exp,
                                             bias=s_lob1[:], scale=SCALE) \
                            .then_inc(sem('cxp'), 1)
                        nc.vector.wait_ge(sem('cxp'), grow + 1)
                        nc.vector.reduce_sum(s_fin[:, 1:2], s_mst[:, 8:16], axis=AX.X)
                        nc.vector.drain()
                        nc.vector.tensor_add(s_fin[:, 1:2], s_fin[:, 1:2], s_fin[:, 3:4])
                        nc.vector.drain()
                        nc.vector.reciprocal(s_fin[:, 2:3], s_fin[:, 1:2])
                        nc.vector.drain()
                        nc.vector.engine_nop().then_inc(sem('fd'), 1)
                        nc.vector.wait_ge(sem('av0'), nbr0 + p0S)
                        nc.vector.wait_ge(sem('av1'), nbr1 + p1S)
                        nc.vector.tensor_scalar_mul(
                            s_yall[:, ds((qt * H + hg) * 64, 64)],
                            ps_y[:, 0:64], s_fin[:, 2:3]).then_inc(sem('rowd'), 1)

        if phase < 5:
            nc.gpsimd.wait_ge(sem('rowd'), H * NQT)
            nc.gpsimd.wait_ge(sem('wld'), wld_cnt[0])
            nc.gpsimd.dma_start(
                out=outT[:, 0:NQT * HD // KT].rearrange("(kt p) n -> p kt n", p=128),
                in_=s_yall[:].rearrange("p (kt n) -> p kt n", kt=KT)) \
                .then_inc(sem('ostg'), 16)
            nc.gpsimd.wait_ge(sem('ostg'), 16)
            close_act()
            nc.compile()
            return nc
        close_act()
        with contextlib.ExitStack() as p6:
            s_yfm = sb("s_yfm", [128, KQ * TQ], BF16, p6)
            s_wo = sb("s_wo", [128, KQ * E], BF16, p6)
            s_yst = sb("s_yst", [128, HD], BF16, p6)
            s_wst = sb("s_wst", [128, KQ * 128], BF16, p6)
            s_zc = sb("s_zc", [128, KT * 512], F32, p6)
            s_zsq = sb("s_zsq", [128, 512], F32, p6)
            s_zcm = sb("s_zcm", [128, 512], F32, p6)
            s_x2c = sb("s_x2c", [128, KT * 512], F32, p6)
            s_srow6 = sb("s_srow6", [1, 1536], F32, p6)
            ps_a6 = ps("ps_a6", [128, 512], F32, p6)
            ps_b6 = ps("ps_b6", [128, 512], F32, p6)
            ps_t6 = ps("ps_t6", [128, 1024], BF16, p6)
            ps_st6 = ps("ps_st6", [1, 1024], F32, p6)
            ps_pq6 = ps("ps_pq6", [128, 1024], F32, p6)

            nc.gpsimd.wait_ge(sem('rowd'), H * NQT)
            wload(s_wo[:].rearrange("p (kt n) -> p kt n", kt=KQ),
                  wo.rearrange("(kt p) n -> p kt n", p=128))
            W_O = wld_cnt[0]

            nc.vector.wait_ge(sem('rowd'), H * NQT)
            with nc.Fori(0, NQT) as qt:
                nc.vector.wait_ge(sem('ytr'), qt)
                nc.vector.tensor_copy(s_yst[:], s_yall[:, ds(qt * HD, HD)]) \
                    .then_inc(sem('ycp'), 1)
                nc.tensor.wait_ge(sem('ycp'), qt + 1)
                nc.tensor.wait_ge(sem('yev'), qt)
                for ft in range(KQ):
                    i2 = nc.tensor.transpose(ps_t6[:, ft * 128:(ft + 1) * 128],
                                             s_yst[:, ft * 128:(ft + 1) * 128],
                                             s_id[:, :])
                    if ft == KQ - 1:
                        i2.then_inc(sem('ytr'), 1)
                nc.vector.wait_ge(sem('ytr'), qt + 1)
                nc.vector.tensor_copy(
                    bass.AP(s_yfm, qt * 128, [[KQ * TQ, 128], [TQ, KQ], [1, 128]]),
                    ps_t6[:, 0:KQ * 128]).then_inc(sem('yev'), 1)

            if phase >= 6:
                with nc.Fori(0, NQC) as c:
                    nc.sync.wait_ge(sem('rowd'), H * NQT)
                    nc.sync.wait_ge(sem('x2st'), c * 16)
                    nc.sync.dma_start(
                        out=s_x2c[:].rearrange("p (kt n) -> p kt n", kt=KT),
                        in_=xoT[:, ds(c * 512, 512)].rearrange("(kt p) n -> p kt n", p=128),
                    ).then_inc(sem('xrl'), 16)
                    with nc.Fori(0, KT) as mt:
                        gcm = c * KT + mt
                        nc.vector.wait_ge(sem('wld'), W_O)
                        nc.vector.wait_ge(sem('womm'), gcm)
                        nc.vector.tensor_copy(
                            s_wst[:].rearrange("p (kt n) -> p kt n", kt=KQ),
                            bass.AP(s_wo, mt * 128, [[KQ * E, 128], [E, KQ], [1, 128]]),
                        ).then_inc(sem('wost'), 1)
                        nc.tensor.wait_ge(sem('wost'), gcm + 1)
                        nc.tensor.wait_ge(sem('yev'), NQT)
                        nc.tensor.wait_ge(sem('zev'), gcm)
                        nc.tensor.wait_ge(sem('zsq2'), gcm)
                        for kt in range(KQ):
                            i2 = nc.tensor.matmul(ps_a6[:, 0:512],
                                                  s_wst[:, kt * 128:(kt + 1) * 128],
                                                  s_yfm[:, ds(kt * TQ + c * 512, 512)],
                                                  start=(kt == 0), stop=(kt == KQ - 1))
                            if kt == KQ - 1:
                                i2.then_inc(sem('womm'), 1)
                        nc.vector.wait_ge(sem('womm'), gcm + 1)
                        nc.vector.tensor_copy(s_zc[:, ds(mt * 512, 512)], ps_a6[:, 0:512]) \
                            .then_inc(sem('zev'), 1)
                        nc.scalar.wait_ge(sem('zev'), gcm + 1)
                        nc.scalar.wait_ge(sem('zsq'), gcm)
                        nc.scalar.activation(s_zsq[:], s_zc[:, ds(mt * 512, 512)],
                                             AF.Square,
                                             bias=0.0, scale=1.0).then_inc(sem('zsq2'), 1)
                        if True:
                            nc.tensor.wait_ge(sem('zsq2'), gcm + 1)
                            nc.tensor.wait_ge(sem('zev'), gcm + 1)
                            nc.tensor.wait_ge(sem('ones'), 1)
                            for bi in nc.Switch(engines=[nc.tensor],
                                                index=[nc.tensor.compute_val(mt == 0)],
                                                n=2):
                                st_ = (bi == 1)
                                nc.tensor.matmul(ps_st6[0:1, 0:512], s_onesf[:, 0:1],
                                                 s_zc[:, ds(mt * 512, 512)],
                                                 start=st_, stop=True,
                                                 skip_group_check=True)
                                nc.tensor.matmul(ps_st6[0:1, 512:1024], s_onesf[:, 0:1],
                                                 s_zsq[:], start=st_, stop=True,
                                                 skip_group_check=True) \
                                    .then_inc(sem('zsq'), 1)
                    if True:
                        emit_ln_math('lnxB', c, s_srow6, ps_st6, ps_pq6,
                                     [('zsq', (c + 1) * KT)], [('x2ap', c * KT)])
                        nc.vector.wait_ge(sem('lnxB'), 4 * c + 4)
                        nc.vector.wait_ge(sem('xrl'), (c + 1) * 16)
                        nc.vector.wait_ge(sem('zev'), (c + 1) * KT)
                        for kt in range(KT):
                            nc.vector.drain()
                            nc.vector.scalar_tensor_tensor(
                                s_zcm[:], s_zc[:, ds(kt * 512, 512)], 1.0, ps_pq6[:, 0:512],
                                ALU.bypass, ALU.mult)
                            nc.vector.tensor_sub(s_x2c[:, ds(kt * 512, 512)],
                                                 s_x2c[:, ds(kt * 512, 512)], ps_pq6[:, 512:1024])
                            nc.vector.drain()
                            nc.vector.tensor_add(s_x2c[:, ds(kt * 512, 512)],
                                                 s_x2c[:, ds(kt * 512, 512)], s_zcm[:]) \
                                .then_inc(sem('x2ap'), 1)
                        nc.sync.wait_ge(sem('x2ap'), (c + 1) * KT)
                        nc.sync.dma_start(
                            out=x2d[:, ds(c * 512, 512)].rearrange("(kt p) n -> p kt n", p=128),
                            in_=s_x2c[:].rearrange("p (kt n) -> p kt n", kt=KT),
                        ).then_inc(sem('x2st'), 16)

        if phase < 7:
            if phase >= 6:
                nc.gpsimd.wait_ge(sem('x2st'), NQC * 16)
            else:
                nc.gpsimd.wait_ge(sem('yev'), NQT)
            nc.gpsimd.dma_start(out=outT[:, 0:TQ].rearrange("(kt p) n -> p kt n", p=128),
                                in_=x2d[:, 0:TQ].rearrange("(kt p) n -> p kt n", p=128)) \
                .then_inc(sem('ostg'), 16)
            nc.gpsimd.wait_ge(sem('ostg'), 16)
            nc.compile()
            return nc
        with contextlib.ExitStack() as p7:
            s_u16 = sb("s_u16", [128, KT * TQH], BF16, p7)
            s_fa = sb("s_fa", [128, KHID * TQH], BF16, p7)
            s_fb = sb("s_fb", [128, KHID * TQH], BF16, p7)
            s_w12 = sb("s_w12", [128, 4 * KT * 128], BF16, p7)
            s_hc = sb("s_hc", [128, 2 * KHID * 128], BF16, p7)
            s_x7 = sb("s_x7", [128, KT * 512], F32, p7)
            s_sq7 = sb("s_sq7", [128, 512], F32, p7)
            s_silu = sb("s_silu", [128, 512], BF16, p7)
            s_zsq7 = sb("s_zsq7", [128, 512], BF16, p7)
            s_fcm = sb("s_fcm", [128, 512], F32, p7)
            s_srow7 = sb("s_srow7", [1, 1536], F32, p7)
            ps_a7 = ps("ps_a7", [128, 512], F32, p7)
            ps_b7 = ps("ps_b7", [128, 512], F32, p7)
            ps_st7 = ps("ps_st7", [1, 1024], F32, p7)
            ps_pq7 = ps("ps_pq7", [128, 1024], F32, p7)

            NMT12 = SCLN // 2
            NMTH = KHID // 2
            NMT3 = KT // 2

            with nc.Fori(0, 2) as half:
                tof = half * TQH

                with nc.Fori(0, NCH) as c:
                    cc7 = half * NCH + c
                    nc.sync.wait_ge(sem('x2st'), NQC * 16)
                    nc.sync.wait_ge(sem('l2ap'), cc7 * KT)
                    nc.sync.wait_ge(sem('oap'), half * NCH * KT)
                    nc.sync.wait_ge(sem('ost'), half * NCH * 16)
                    nc.sync.dma_start(
                        out=s_x7[:].rearrange("p (kt n) -> p kt n", kt=KT),
                        in_=x2d[:, ds(tof + c * 512, 512)].rearrange("(kt p) n -> p kt n", p=128),
                    ).then_inc(sem('xl7'), 16)
                    with nc.Fori(0, KT) as ktl:
                        gl = cc7 * KT + ktl
                        nc.scalar.wait_ge(sem('xl7'), (cc7 + 1) * 16)
                        nc.scalar.wait_ge(sem('st7'), gl)
                        nc.scalar.activation(s_sq7[:], s_x7[:, ds(ktl * 512, 512)],
                                             AF.Square, bias=0.0, scale=1.0) \
                            .then_inc(sem('xq7'), 1)
                        nc.tensor.wait_ge(sem('xl7'), (cc7 + 1) * 16)
                        nc.tensor.wait_ge(sem('xq7'), gl + 1)
                        nc.tensor.wait_ge(sem('st7d'), cc7)
                        nc.tensor.wait_ge(sem('fsd'), half * NCH)
                        for bi in nc.Switch(engines=[nc.tensor],
                                            index=[nc.tensor.compute_val(ktl == 0)], n=2):
                            st_ = (bi == 1)
                            nc.tensor.matmul(ps_st7[0:1, 0:512], s_onesf[:, 0:1],
                                             s_x7[:, ds(ktl * 512, 512)],
                                             start=st_, stop=True, skip_group_check=True)
                            nc.tensor.matmul(ps_st7[0:1, 512:1024], s_onesf[:, 0:1],
                                             s_sq7[:], start=st_, stop=True,
                                             skip_group_check=True).then_inc(sem('st7'), 1)
                    emit_ln_math('lnxC', cc7, s_srow7, ps_st7, ps_pq7,
                                 [('st7', (cc7 + 1) * KT)],
                                 [('l2ap', cc7 * KT), ('fsd', half * NCH)])
                    nc.vector.engine_nop().then_inc(sem('st7d'), 1)
                    nc.vector.wait_ge(sem('lnxC'), 4 * cc7 + 4)
                    for kt in range(KT):
                        nc.vector.scalar_tensor_tensor(
                            s_fcm[:], s_x7[:, ds(kt * 512, 512)], 1.0,
                            ps_pq7[:, 0:512], ALU.bypass, ALU.mult)
                        nc.vector.drain()
                        nc.vector.tensor_sub(s_u16[:, ds(kt * TQH + c * 512, 512)],
                                             s_fcm[:], ps_pq7[:, 512:1024]) \
                            .then_inc(sem('l2ap'), 1)
                        nc.vector.drain()

                def w12dma(wsrc, mt_expr, slot, ordv):
                    nc.gpsimd.wait_ge(sem('f12%d' % slot), ordv * 16)
                    nc.gpsimd.dma_start(
                        out=s_w12[:, slot * 1024:(slot + 1) * 1024]
                        .rearrange("p (kt n) -> p kt n", kt=KT),
                        in_=wsrc[:, ds(mt_expr * 128, 128)].rearrange("(kt p) n -> p kt n", p=128),
                    ).then_inc(sem('f12%d' % slot), 16)

                nc.gpsimd.wait_ge(sem('x2st'), NQC * 16)
                for wpar in (0, 1):
                    w12dma(w1t, wpar, wpar, half * NMT12)
                    w12dma(w2t, wpar, wpar + 2, half * NMT12)
                with nc.Fori(0, NMT12) as mtp:
                    for wpar in (0, 1):
                        mt = 2 * mtp + wpar
                        gm = half * SCLN + mt
                        nc.vector.wait_ge(sem('cst'), 96)
                        nc.vector.wait_ge(sem('f3m'), half * KT * NCH)
                        nc.vector.wait_ge(sem('sil'), smax((gm - 1) * NCH + 1, 0))
                        nc.vector.tensor_copy(s_stg[:, 1 + wpar:2 + wpar],
                                              s_scl[:, ds(mt, 1)]) \
                            .then_inc(sem('sstg'), 1)
                        for nch in range(NCH):
                            gmn = gm * NCH + nch
                            nc.tensor.wait_ge(sem('f12%d' % wpar),
                                              (half * NMT12 + mtp + 1) * 16)
                            nc.tensor.wait_ge(sem('f12%d' % (wpar + 2)),
                                              (half * NMT12 + mtp + 1) * 16)
                            nc.tensor.wait_ge(sem('l2ap'), (half + 1) * NCH * KT)
                            nc.tensor.wait_ge(sem('sil'), gmn)
                            nc.tensor.wait_ge(sem('gate'), gmn)
                            for kt in range(KT):
                                i2 = nc.tensor.matmul(
                                    ps_a7[:, 0:512],
                                    s_w12[:, wpar * 1024 + kt * 128:wpar * 1024 + kt * 128 + 128],
                                    s_u16[:, ds(kt * TQH + nch * 512, 512)],
                                    start=(kt == 0), stop=(kt == KT - 1))
                            for kt in range(KT):
                                i2 = nc.tensor.matmul(
                                    ps_b7[:, 0:512],
                                    s_w12[:, (wpar + 2) * 1024 + kt * 128:
                                          (wpar + 2) * 1024 + kt * 128 + 128],
                                    s_u16[:, ds(kt * TQH + nch * 512, 512)],
                                    start=(kt == 0), stop=(kt == KT - 1))
                                if kt == KT - 1:
                                    i2.then_inc(sem('w12mm'), 1)
                            nc.scalar.wait_ge(sem('w12mm'), gmn + 1)
                            nc.scalar.wait_ge(sem('sstg'), gm + 1)
                            nc.scalar.wait_ge(sem('gate'), gmn)
                            nc.scalar.activation(s_silu[:], ps_a7[:, 0:512], AF.Silu,
                                                 bias=0.0,
                                                 scale=s_stg[:, 1 + wpar:2 + wpar]) \
                                .then_inc(sem('sil'), 1)
                            nc.vector.wait_ge(sem('sil'), gmn + 1)
                            nc.vector.scalar_tensor_tensor(
                                s_fa[:, ds(mt * TQH + nch * 512, 512)],
                                ps_b7[:, 0:512], s_scl[:, ds(SCLN + mt, 1)],
                                s_silu[:], ALU.mult, ALU.mult).then_inc(sem('gate'), 1)
                        with nc.gpsimd.If_lt(nc.gpsimd.to_reg(mt), SCLN - 2):
                            nc.gpsimd.wait_ge(sem('w12mm'), (gm + 1) * NCH)
                            w12dma(w1t, smin(mt + 2, SCLN - 1), wpar,
                                   half * NMT12 + mtp + 1)
                            w12dma(w2t, smin(mt + 2, SCLN - 1), wpar + 2,
                                   half * NMT12 + mtp + 1)

                def hstage(wsrc, src_sb, dst_sb, stgcol, sclbase, phname,
                           prev_sem, prev_val):
                    phm = sem(phname + 'm')
                    phl0 = sem(phname + 'l0')
                    phl1 = sem(phname + 'l1')
                    phs = sem(phname + 's')
                    phe = sem(phname + 'e')
                    phls = (phl0, phl1)

                    def hdma(mt_expr, slot, ordv):
                        nc.gpsimd.wait_ge(phls[slot], ordv * 16)
                        nc.gpsimd.dma_start(
                            out=s_hc[:, slot * KHID * 128:(slot + 1) * KHID * 128]
                            .rearrange("p (kt n) -> p kt n", kt=KHID),
                            in_=wsrc[:, ds(mt_expr * 128, 128)]
                            .rearrange("(kt p) n -> p kt n", p=128),
                        ).then_inc(phls[slot], 16)

                    for wpar in (0, 1):
                        nc.gpsimd.wait_ge(prev_sem, prev_val)
                        hdma(wpar, wpar, half * NMTH)
                    with nc.Fori(0, NMTH) as mtp:
                        for wpar in (0, 1):
                            mt = 2 * mtp + wpar
                            gm = half * KHID + mt
                            nc.vector.wait_ge(sem('cst'), 96)
                            nc.vector.wait_ge(phe, smax((gm - 1) * NCH, 0))
                            nc.vector.tensor_copy(s_stg[:, stgcol + wpar:stgcol + wpar + 1],
                                                  s_scl[:, ds(sclbase + mt, 1)]) \
                                .then_inc(phs, 1)
                            for nch in range(NCH):
                                gmn = gm * NCH + nch
                                nc.tensor.wait_ge(phls[wpar],
                                                  (half * NMTH + mtp + 1) * 16)
                                nc.tensor.wait_ge(phe, smax(gmn - 1, 0))
                                pst = ps_a7 if (nch & 1) == 0 else ps_b7
                                for kt in range(KHID):
                                    i2 = nc.tensor.matmul(
                                        pst[:, 0:512],
                                        s_hc[:, wpar * KHID * 128 + kt * 128:
                                              wpar * KHID * 128 + kt * 128 + 128],
                                        src_sb[:, ds(kt * TQH + nch * 512, 512)],
                                        start=(kt == 0), stop=(kt == KHID - 1))
                                    if kt == KHID - 1:
                                        i2.then_inc(phm, 1)
                                nc.scalar.wait_ge(phm, gmn + 1)
                                nc.scalar.wait_ge(phs, gm + 1)
                                nc.scalar.activation(
                                    dst_sb[:, ds(mt * TQH + nch * 512, 512)],
                                    pst[:, 0:512], AF.Gelu,
                                    bias=0.0,
                                    scale=s_stg[:, stgcol + wpar:stgcol + wpar + 1]) \
                                    .then_inc(phe, 1)
                            with nc.gpsimd.If_lt(nc.gpsimd.to_reg(mt), KHID - 2):
                                nc.gpsimd.wait_ge(phm, (gm + 1) * NCH)
                                hdma(smin(mt + 2, KHID - 1), wpar,
                                     half * NMTH + mtp + 1)

                nc.tensor.wait_ge(sem('gate'), (half + 1) * SCLN * NCH)
                hstage(h1t, s_fa, s_fb, 3, 2 * SCLN, 'h1', sem('f3m'), half * KT * NCH)
                nc.tensor.wait_ge(sem('h1e'), (half + 1) * KHID * NCH)
                hstage(h2t, s_fb, s_fa, 5, 3 * SCLN, 'h2', sem('h1m'), (half + 1) * KHID * NCH)

                nc.tensor.wait_ge(sem('h2e'), (half + 1) * KHID * NCH)
                def w3dma(mt_expr, slot, ordv):
                    sname = 'f3l%d' % slot
                    nc.gpsimd.wait_ge(sem(sname), ordv * 16)
                    nc.gpsimd.dma_start(
                        out=s_hc[:, slot * KHID * 128:(slot + 1) * KHID * 128]
                        .rearrange("p (kt n) -> p kt n", kt=KHID),
                        in_=w3t[:, ds(mt_expr * 128, 128)].rearrange("(kt p) n -> p kt n", p=128),
                    ).then_inc(sem(sname), 16)

                for wpar in (0, 1):
                    nc.gpsimd.wait_ge(sem('h2m'), (half + 1) * KHID * NCH)
                    w3dma(wpar, wpar, half * NMT3)
                with nc.Fori(0, NMT3) as mtp:
                    for wpar in (0, 1):
                        mt = 2 * mtp + wpar
                        gm = half * KT + mt
                        for nch in range(NCH):
                            gmn = gm * NCH + nch
                            nc.tensor.wait_ge(sem('f3l%d' % wpar),
                                              (half * NMT3 + mtp + 1) * 16)
                            nc.tensor.wait_ge(sem('f3e'), smax(gmn - 1, 0))
                            pst = ps_a7 if (nch & 1) == 0 else ps_b7
                            for kt in range(KHID):
                                i2 = nc.tensor.matmul(
                                    pst[:, 0:512],
                                    s_hc[:, wpar * KHID * 128 + kt * 128:
                                          wpar * KHID * 128 + kt * 128 + 128],
                                    s_fa[:, ds(kt * TQH + nch * 512, 512)],
                                    start=(kt == 0), stop=(kt == KHID - 1))
                                if kt == KHID - 1:
                                    i2.then_inc(sem('f3m'), 1)
                            nc.vector.wait_ge(sem('f3m'), gmn + 1)
                            nc.vector.wait_ge(sem('oap'), half * NCH * KT)
                            nc.vector.tensor_scalar_mul(
                                s_u16[:, ds(mt * TQH + nch * 512, 512)],
                                pst[:, 0:512], s_scl[:, ds(4 * SCLN + mt, 1)]) \
                                .then_inc(sem('f3e'), 1)
                        with nc.gpsimd.If_lt(nc.gpsimd.to_reg(mt), KT - 2):
                            nc.gpsimd.wait_ge(sem('f3m'), (gm + 1) * NCH)
                            w3dma(smin(mt + 2, KT - 1), wpar,
                                  half * NMT3 + mtp + 1)

                with nc.Fori(0, NCH) as c:
                    cf = half * NCH + c
                    nc.sync.wait_ge(sem('oap'), cf * KT)
                    nc.sync.wait_ge(sem('ost'), cf * 16)
                    nc.sync.wait_ge(sem('l2ap'), (half + 1) * NCH * KT)
                    nc.sync.dma_start(
                        out=s_x7[:].rearrange("p (kt n) -> p kt n", kt=KT),
                        in_=x2d[:, ds(tof + c * 512, 512)].rearrange("(kt p) n -> p kt n", p=128),
                    ).then_inc(sem('xf7'), 16)
                    nc.tensor.wait_ge(sem('f3e'), (half + 1) * KT * NCH)
                    nc.tensor.wait_ge(sem('fsd'), cf)
                    nc.tensor.wait_ge(sem('st7d'), (half + 1) * NCH)
                    with nc.Fori(0, KT) as kt7:
                        gz = cf * KT + kt7
                        nc.scalar.wait_ge(sem('fst'), gz)
                        nc.scalar.wait_ge(sem('f3e'), (half + 1) * KT * NCH)
                        nc.scalar.activation(s_zsq7[:],
                                             s_u16[:, ds(kt7 * TQH + c * 512, 512)],
                                             AF.Square, bias=0.0, scale=1.0) \
                            .then_inc(sem('fsq'), 1)
                        nc.tensor.wait_ge(sem('fsq'), gz + 1)
                        for bi in nc.Switch(engines=[nc.tensor],
                                            index=[nc.tensor.compute_val(kt7 == 0)], n=2):
                            st_ = (bi == 1)
                            nc.tensor.matmul(ps_st7[0:1, 0:512], s_ones[:, 0:1],
                                             s_u16[:, ds(kt7 * TQH + c * 512, 512)],
                                             start=st_, stop=True, skip_group_check=True)
                            nc.tensor.matmul(ps_st7[0:1, 512:1024], s_ones[:, 0:1],
                                             s_zsq7[:], start=st_, stop=True,
                                             skip_group_check=True).then_inc(sem('fst'), 1)

                    emit_ln_math('lnxD', cf, s_srow7, ps_st7, ps_pq7,
                                 [('fst', (cf + 1) * KT)],
                                 [('oap', cf * KT), ('l2ap', (half + 1) * NCH * KT)])
                    nc.vector.wait_ge(sem('lnxD'), 4 * cf + 4)
                    nc.vector.wait_ge(sem('xf7'), (cf + 1) * 16)
                    nc.vector.wait_ge(sem('f3e'), (half + 1) * KT * NCH)
                    for kt in range(KT):
                        nc.vector.drain()
                        nc.vector.scalar_tensor_tensor(
                            s_fcm[:], s_u16[:, ds(kt * TQH + c * 512, 512)], 1.0,
                            ps_pq7[:, 0:512], ALU.bypass, ALU.mult)
                        nc.vector.tensor_sub(s_x7[:, ds(kt * 512, 512)],
                                             s_x7[:, ds(kt * 512, 512)],
                                             ps_pq7[:, 512:1024])
                        nc.vector.drain()
                        nc.vector.tensor_add(s_x7[:, ds(kt * 512, 512)],
                                             s_x7[:, ds(kt * 512, 512)], s_fcm[:]) \
                            .then_inc(sem('oap'), 1)
                    nc.vector.engine_nop().then_inc(sem('fsd'), 1)
                    nc.sync.wait_ge(sem('oap'), (cf + 1) * KT)
                    nc.sync.dma_start(
                        out=outT[:, ds(tof + c * 512, 512)].rearrange("(kt p) n -> p kt n", p=128),
                        in_=s_x7[:].rearrange("p (kt n) -> p kt n", kt=KT),
                    ).then_inc(sem('ost'), 16)

            nc.sync.wait_ge(sem('ost'), NCH * 16)
            nc.gpsimd.wait_ge(sem('ost'), NCH * 16)

    nc.compile()
    return nc



def host_prepare(cfg: Cfg, inputs, batch, parity):
    TK, TQ, H, DHID = cfg.TK, cfg.TQ, cfg.H, cfg.DHID
    NQT = cfg.NQT
    x = np.asarray(inputs["L0_attn_x"])[batch]
    pos = np.concatenate([np.arange(128 * (2 * j + parity), 128 * (2 * j + parity) + 128)
                          for j in range(NQT)])
    xf = np.ascontiguousarray(x[:TK].T.astype(np.float32))
    xo = np.ascontiguousarray(x[pos].T.astype(np.float32))

    inv = 1.0 / 10000.0 ** (np.arange(16, dtype=np.float32) / 16.0)

    def qtabs(positions):
        ang = positions[:, None].astype(np.float32) * inv[None, :]
        c = np.cos(ang).T
        s = np.sin(ang).T
        cos128 = np.zeros((128, len(positions)), np.float32)
        sin128 = np.zeros((128, len(positions)), np.float32)
        for r in range(128):
            d = r % 32
            cos128[r] = c[d % 16]
            sin128[r] = -s[d] if d < 16 else s[d - 16]
        return cos128, sin128

    cq, sq = qtabs(pos)
    angk = np.arange(TK, dtype=np.float32)[:, None] * inv[None, :]
    ck = np.cos(angk).T
    skt = np.sin(angk).T
    cosk = np.zeros((32, TK), np.float32)
    sink = np.zeros((32, TK), np.float32)
    for d in range(32):
        cosk[d] = ck[d % 16]
        sink[d] = -skt[d] if d < 16 else skt[d - 16]

    mt = np.zeros((128, 1536), np.float32)
    tril = np.where(np.arange(128)[:, None] >= np.arange(128)[None, :], 0.0, NEG)
    if parity == 0:
        mt[:, 0:128] = tril
        mt[:, 128:256] = NEG
    else:
        mt[:, 0:128] = 0.0
        mt[:, 128:256] = tril
    mt[:, 256:1280] = NEG

    tile_scl = lambda v, n: np.ascontiguousarray(
        np.asarray(v, np.float32).reshape(n, 128).T)

    d = {
        "xfT": xf, "xoT": xo,
        "wq": np.ascontiguousarray(np.asarray(inputs["L0_attn_Wq"], np.float32)),
        "wqr": np.ascontiguousarray(np.asarray(inputs["L0_attn_Wqr"], np.float32)),
        "wkr": np.ascontiguousarray(np.asarray(inputs["L0_attn_Wkr"], np.float32)),
        "wkvd": np.ascontiguousarray(np.asarray(inputs["L0_attn_Wkvd"], np.float32)),
        "wku": np.ascontiguousarray(np.asarray(inputs["L0_attn_Wku"], np.float32)),
        "wvu": np.ascontiguousarray(np.asarray(inputs["L0_attn_Wvu"], np.float32)),
        "wo": np.ascontiguousarray(np.asarray(inputs["L0_attn_Wo"], np.float32)),
        "w1t": np.ascontiguousarray(np.asarray(inputs["L0_ffn_w1_w"], np.float32).T),
        "w2t": np.ascontiguousarray(np.asarray(inputs["L0_ffn_w2_w"], np.float32).T),
        "h1t": np.ascontiguousarray(np.asarray(inputs["L0_ffn_h1_w"], np.float32).T),
        "h2t": np.ascontiguousarray(np.asarray(inputs["L0_ffn_h2_w"], np.float32).T),
        "w3t": np.ascontiguousarray(np.asarray(inputs["L0_ffn_w3_w"], np.float32).T),
        "s1s": tile_scl(inputs["L0_ffn_w1_s"], DHID // 128),
        "s2s": tile_scl(inputs["L0_ffn_w2_s"], DHID // 128),
        "h1ss": tile_scl(inputs["L0_ffn_h1_s"], DHID // 128),
        "h2ss": tile_scl(inputs["L0_ffn_h2_s"], DHID // 128),
        "s3s": tile_scl(inputs["L0_ffn_w3_s"], KT),
        "lobo": np.ascontiguousarray(np.broadcast_to(
            np.asarray(inputs["L0_attn_lobo_log"], np.float32)[None, :], (128, H)).copy()),
        "ident": np.eye(128, dtype=np.float32),
        "masktab": mt,
        "cosq": cq, "sinq": sq, "cosk": cosk, "sink": sink,
    }
    return d, pos


_NC_CACHE = {}


def kernel(**inputs):
    cfg = Cfg()
    key = (cfg.TK, cfg.TQ, cfg.H, cfg.DHID)
    if key not in _NC_CACHE:
        _NC_CACHE[key] = build_nc(cfg)
    nc = _NC_CACHE[key]
    in_maps = []
    poss = []
    for core in range(8):
        d, pos = host_prepare(cfg, inputs, core // 2, core % 2)
        in_maps.append(d)
        poss.append(pos)
    res = run_bass_kernel_spmd(nc, in_maps, core_ids=list(range(8)))
    out = np.zeros((4, 4096, 1024), dtype=np.float32)
    for core in range(8):
        out[core // 2][poss[core]] = res.results[core]["outT"].T
    return out
